# version 05
# baseline (speedup 1.0000x reference)
import contextlib

import numpy as np
import ml_dtypes

import concourse.bass as bass
import concourse.mybir as mybir
from concourse.bass_utils import run_bass_kernel_spmd

F32 = mybir.dt.float32
BF16 = mybir.dt.bfloat16
AF = mybir.ActivationFunctionType
ALU = mybir.AluOpType
AX = mybir.AxisListType
BF = ml_dtypes.bfloat16

NCORE = 8
NSEQ = 2
T = 2048
D = 2048
H = 32
DH = 64
CH = 32
C = 64
NPAIR = 16
NT = 16
ND = 16
EPS_LN = 1e-5
EPS_L2 = 1e-6
H1 = 16
DH1 = 128
KVH = 4
E_M50 = float(np.exp(np.float64(-50.0)))


def _rope_perm():
    perm = np.zeros(D, dtype=np.int64)
    for h in range(H):
        for j in range(32):
            perm[32 * h + j] = 64 * h + 2 * j
            perm[1024 + 32 * h + j] = 64 * h + 2 * j + 1
    return perm


def _yarn_tables():
    ROPE_BASE, SCALING, BETA_FAST, BETA_SLOW = 10000.0, 32.0, 1.0, 32.0
    half = np.arange(0, DH, 2, dtype=np.float32) / DH
    inv_freq = 1.0 / ROPE_BASE ** half
    wavelen = 2.0 * np.pi / inv_freq
    ramp = np.clip((wavelen - BETA_FAST) / (BETA_SLOW - BETA_FAST), 0.0, 1.0)
    scale = 1.0 + (SCALING - 1.0) * ramp
    t = np.arange(T, dtype=np.float32)
    freqs = (t[:, None] / scale[None, :]) * inv_freq[None, :]
    idx = (2 * np.arange(32)) % 32
    cos = np.cos(freqs[:, idx]).astype(np.float32)
    sin = np.sin(freqs[:, idx]).astype(np.float32)
    return np.tile(cos.T, (4, 1)), np.tile(sin.T, (4, 1))


def _const_arrays():
    c = {}
    c["IDENT"] = np.eye(128, dtype=BF)
    c["IDENTF"] = np.eye(128, dtype=np.float32)
    cos_fm, sin_fm = _yarn_tables()
    c["COS"], c["SIN"] = cos_fm, sin_fm
    bo = np.zeros((128, 4), dtype=np.float32)
    for u in range(4):
        bo[32 * u:32 * u + 32, u] = 1.0
    c["ONES32G"] = bo
    sel = np.zeros((4, 128), dtype=np.float32)
    for u in range(4):
        sel[u, 32 * u:32 * u + 32] = 1.0
    c["SEL32F"] = sel
    return c


def prep_stageA(inputs):
    perm = _rope_perm()
    f32 = np.float32
    d = dict(_const_arrays())
    A = lambda k: np.asarray(inputs[k])
    d["WQT"] = np.ascontiguousarray(A("L0_attn_Wq")[perm, :].T).astype(f32)
    d["WKT"] = np.ascontiguousarray(A("L0_attn_Wk")[perm, :].T).astype(f32)
    d["WVT"] = np.ascontiguousarray(A("L0_attn_Wv").T).astype(f32)
    d["WGT"] = np.ascontiguousarray(A("L0_attn_Wg").T).astype(f32)
    d["WBT"] = np.ascontiguousarray(A("L0_attn_Wb").T).astype(f32)
    d["WGKT"] = np.ascontiguousarray(A("L0_attn_Wgk").T).astype(f32)
    d["QCONV"] = np.ascontiguousarray(A("L0_attn_wq_conv")[perm]).astype(f32)
    d["KCONV"] = np.ascontiguousarray(A("L0_attn_wk_conv")[perm]).astype(f32)
    d["VCONV"] = np.ascontiguousarray(A("L0_attn_wv_conv")).astype(f32)
    d["QCB"] = A("L0_attn_bq_conv")[perm].reshape(D, 1).astype(f32)
    d["KCB"] = A("L0_attn_bk_conv")[perm].reshape(D, 1).astype(f32)
    d["VCB"] = A("L0_attn_bv_conv").reshape(D, 1).astype(f32)
    d["BB"] = A("L0_attn_bb").reshape(H, 1).astype(f32)
    d["BGK"] = A("L0_attn_bgk").reshape(H, 1).astype(f32)
    d["ALOG"] = A("L0_attn_A_log").reshape(H, 1).astype(f32)
    d["DTB"] = A("L0_attn_dt_bias").reshape(H, 1).astype(f32)
    return d


STAGEA_IN = ["X", "WQT", "WKT", "WVT", "WGT", "WBT", "WGKT", "QCONV", "KCONV",
             "VCONV", "QCB", "KCB", "VCB", "BB", "BGK", "ALOG", "DTB",
             "IDENT", "IDENTF", "COS", "SIN", "ONES32G", "SEL32F"]
STAGEA_OUT = ["XN_TOK", "QF", "KF", "KT_PAIR", "VT_PAIR", "VFM", "VT", "GT",
              "BETAFM", "CUMFM"]



class Sem:
    def __init__(self, stack, nc, name):
        self.h = stack.enter_context(nc.semaphore(name))
        self.n = 0

    def inc(self, instr, k=1):
        instr.then_inc(self.h, k)
        self.n += k
        return instr

    def wait(self, eng, lag=0):
        eng.wait_ge(self.h, max(0, self.n - lag))


class Prog:
    def __init__(self):
        self.nc = bass.Bass("TRN2", target_bir_lowering=False, detect_race_conditions=False)
        self.t = {}
        self.stack = contextlib.ExitStack()
        self._scopes = []

    def push_scope(self):
        if not hasattr(self, "_scope_ctr"):
            self._scope_ctr = 0
        self._scope_ctr += 1
        self._scopes.append((contextlib.ExitStack(), {}))

    def pop_scope(self):
        st, cache = self._scopes.pop()
        st.close()

    def din(self, name, shape, dtype=F32):
        self.t[name] = self.nc.dram_tensor(name, list(shape), dtype, kind="ExternalInput")
        return self.t[name]

    def dout(self, name, shape, dtype=F32):
        self.t[name] = self.nc.dram_tensor(name, list(shape), dtype, kind="ExternalOutput")
        return self.t[name]

    def dscratch(self, name, shape, dtype=F32):
        self.t[name] = self.nc.dram_tensor(name, list(shape), dtype)
        return self.t[name]

    def sb(self, name, shape, dtype=F32):
        if not hasattr(self, "_sbcache"):
            self._sbcache = {}
        if name in self._sbcache:
            return self._sbcache[name]
        for st, cache in self._scopes:
            if name in cache:
                return cache[name]
        if self._scopes:
            st, cache = self._scopes[-1]
            uniq = "z%d_%s" % (self._scope_ctr, name)
            cache[name] = st.enter_context(self.nc.sbuf_tensor(uniq, list(shape), dtype))
            return cache[name]
        self._sbcache[name] = self.stack.enter_context(
            self.nc.sbuf_tensor(name, list(shape), dtype))
        return self._sbcache[name]

    def ps(self, name, shape, dtype=F32):
        if not hasattr(self, "_sbcache"):
            self._sbcache = {}
        if name not in self._sbcache:
            self._sbcache[name] = self.stack.enter_context(
                self.nc.psum_tensor(name, list(shape), dtype))
        return self._sbcache[name]

    def shared_pool(self, with_scr=True):
        self.sb("a_xnfm", (128, ND * 2048), BF16)
        self.sb("a_wbuf", (128, 4096), BF16)
        self.sb("a_id", (128, 128), BF16)
        self.sb("a_idf", (128, 128), F32)
        self.sb("b2_ones1", (1, 128), F32)
        self.sb("rowbuf", (1, 512), F32)
        if with_scr:
            for i in range(6):
                self.sb("scr%d" % i, (128, 2052), F32)
        for n in ("PSA", "PSB", "PSC", "PSD", "PSE", "PSF"):
            self.ps(n, (128, 512), F32)
        self.ps("PST", (128, 1024), BF16)

    def sem(self, name):
        return Sem(self.stack, self.nc, name)

    def consts(self, values, dtype=F32):
        vals = sorted(set(float(v) for v in values))
        tile = self.sb("constpool", (128, len(vals)), dtype)
        with self.nc.Block() as block:
            @block.gpsimd
            def _(g):
                for j, v in enumerate(vals):
                    g.memset(tile[:, j:j + 1], v)
                    self.nc.const_aps.register(dtype, v, tile[:, j:j + 1])

    def close(self):
        self.stack.close()


def declare_stageA(p, out):
    p.din("X", (NSEQ, T, D))
    p.din("WQT", (D, D)); p.din("WKT", (D, D)); p.din("WVT", (D, D)); p.din("WGT", (D, D))
    p.din("WBT", (D, H)); p.din("WGKT", (D, H))
    p.din("QCONV", (D, 4)); p.din("KCONV", (D, 4)); p.din("VCONV", (D, 4))
    p.din("QCB", (D, 1)); p.din("KCB", (D, 1)); p.din("VCB", (D, 1))
    p.din("BB", (H, 1)); p.din("BGK", (H, 1)); p.din("ALOG", (H, 1)); p.din("DTB", (H, 1))
    p.din("IDENT", (128, 128), BF16)
    p.din("IDENTF", (128, 128), F32)
    p.din("COS", (128, T)); p.din("SIN", (128, T))
    p.din("ONES32G", (128, 4), F32); p.din("SEL32F", (4, 128), F32)
    out("XN_TOK", (NSEQ, T, D), F32)
    out("QF", (NSEQ, H, DH, T), BF16)
    out("KF", (NSEQ, H, DH, T), BF16)
    out("KT_PAIR", (NSEQ, CH, NPAIR, 128, DH), BF16)
    out("VT_PAIR", (NSEQ, CH, NPAIR, 128, DH), BF16)
    out("VFM", (NSEQ, D, T), BF16)
    out("VT", (NSEQ, T, D), BF16)
    out("GT", (NSEQ, T, D), BF16)
    out("BETAFM", (NSEQ, H, T), F32)
    out("CUMFM", (NSEQ, H, T), F32)


class StageA:
    def __init__(self, p):
        self.p = p
        self.xn_fm = p.sb("a_xnfm", (128, ND * 2048), BF16)
        self.wbuf = p.sb("a_wbuf", (128, 4096), BF16)
        self.scr = [p.sb("scr%d" % i, (128, 2052), F32) for i in range(6)]
        self.ident = p.sb("a_id", (128, 128), BF16)
        self.identf = p.sb("a_idf", (128, 128), F32)
        self.cosb = p.sb("a_cos", (128, T), BF16)
        self.sinb = p.sb("a_sin", (128, T), BF16)
        self.ones32 = p.sb("a_o32", (128, 4), F32)
        self.sel32 = p.sb("a_sel", (4, 128), F32)
        self.s_ld = p.sem("a_ld")
        self.s_gld = p.sem("a_gld")
        self.s_v = p.sem("a_v")
        self.s_sc = p.sem("a_sc")
        self.s_pe = p.sem("a_pe")
        self.s_st = p.sem("a_st")
        self.s_g = p.sem("a_g")

    def load_consts(self):
        p, nc = self.p, self.p.nc
        with nc.Block() as block:
            @block.sync
            def _(sy):
                for dst, src in ((self.ident, "IDENT"), (self.identf, "IDENTF"),
                                 (self.ones32, "ONES32G"), (self.sel32, "SEL32F")):
                    self.s_ld.inc(sy.dma_start(dst[:], p.t[src][:]), 16)

            @block.gpsimd
            def _(g):
                self.s_gld.inc(g.dma_start(self.cosb[:], p.t["COS"][:]), 16)
                self.s_gld.inc(g.dma_start(self.sinb[:], p.t["SIN"][:]), 16)

    def ln(self, s):
        p, nc = self.p, self.p.nc
        t = p.t
        xt = self.scr[0]
        scr = self.scr[1]
        xnt = self.scr[2]
        xnb = p.sb("ln_xnb", (128, D), BF16)
        ssum = p.sb("ln_s", (128, 1), F32)
        ssq = p.sb("ln_sq", (128, 1), F32)
        rstd = p.sb("ln_rstd", (128, 1), F32)
        nmean = p.sb("ln_nm", (128, 1), F32)
        vtmp = p.sb("ln_v", (128, 1), F32)
        pst = p.ps("PST", (128, 1024), BF16)
        s_ld, s_v, s_sc, s_pe, s_st = self.s_ld, self.s_v, self.s_sc, self.s_pe, self.s_st

        for tt in range(NT):
            with nc.Block() as block:
                @block.sync
                def _(sy):
                    s_ld.inc(sy.dma_start(xt[:, 0:D], t["X"][s, 128 * tt:128 * (tt + 1), :]), 16)

                @block.vector
                def _(v):
                    s_v.inc(v.memset(ssq[:], 0.0))

                @block.scalar
                def _(sc):
                    s_ld.wait(sc); self.s_gld.wait(sc)
                    s_v.wait(sc)
                    s_sc.inc(sc.activation(scr[:, 0:D], xt[:, 0:D], AF.Square, accum_out=ssq[:]))

                @block.vector
                def _(v):
                    s_ld.wait(v); self.s_gld.wait(v)
                    s_v.inc(v.reduce_sum(ssum[:], xt[:, 0:D], axis=AX.X))
                    s_v.wait(v)
                    s_v.inc(v.tensor_scalar_mul(ssum[:], ssum[:], 1.0 / D))
                    s_v.wait(v)
                    s_sc.wait(v)
                    s_v.inc(v.tensor_scalar_mul(ssq[:], ssq[:], 1.0 / D))
                    s_v.inc(v.tensor_mul(vtmp[:], ssum[:], ssum[:]))
                    s_v.wait(v)
                    s_v.inc(v.tensor_sub(vtmp[:], ssq[:], vtmp[:]))

                @block.scalar
                def _(sc):
                    s_v.wait(sc)
                    s_sc.inc(sc.activation(vtmp[:], vtmp[:], AF.Ln, bias=EPS_LN))
                    s_sc.wait(sc)
                    s_sc.inc(sc.activation(rstd[:], vtmp[:], AF.Exp, scale=-0.5))

                @block.vector
                def _(v):
                    s_sc.wait(v)
                    s_v.inc(v.tensor_mul(nmean[:], ssum[:], rstd[:]))
                    s_v.wait(v)
                    v.tensor_scalar(xnt[:, 0:D], xt[:, 0:D], rstd[:], nmean[:],
                                    ALU.mult, ALU.subtract)
                    s_v.inc(v.tensor_copy(xnb[:], xnt[:, 0:D]))

                @block.sync
                def _(sy):
                    s_v.wait(sy)
                    s_st.inc(sy.dma_start(t["XN_TOK"][s, 128 * tt:128 * (tt + 1), :], xnt[:, 0:D]), 16)

            for rnd in range(2):
                with nc.Block() as block:
                    @block.tensor
                    def _(te):
                        s_v.wait(te)
                        for half in range(8):
                            i = 8 * rnd + half
                            ins = te.transpose(pst[:, 128 * half:128 * (half + 1)],
                                               xnb[:, 128 * i:128 * (i + 1)], self.ident[:])
                        s_pe.inc(ins)

                with nc.Block() as block:
                    @block.vector
                    def _(v):
                        s_pe.wait(v)
                        for half in range(8):
                            i = 8 * rnd + half
                            v.tensor_copy(
                                self.xn_fm[:, 2048 * i + 128 * tt:2048 * i + 128 * (tt + 1)],
                                pst[:, 128 * half:128 * (half + 1)])
                        s_v.inc(v.engine_nop())

    def beta_g(self, s):
        p, nc = self.p, self.p.nc
        t = p.t
        s_ld, s_v, s_sc, s_pe, s_st = self.s_ld, self.s_v, self.s_sc, self.s_pe, self.s_st
        wsb = self.wbuf
        bcol = p.sb("bg_bc", (H, 1), F32)
        bcol2 = p.sb("bg_bc2", (H, 1), F32)
        acol = p.sb("bg_ac", (H, 1), F32)
        nea = p.sb("bg_ea", (H, 1), F32)
        zf = self.scr[0]
        zab = self.scr[1]
        sp = self.scr[2]
        out_fm = self.scr[3]
        onesrow = p.sb("bg_ones", (H, C), F32)
        psz = p.ps("PSA", (128, 512), F32)

        for which in ("beta", "g"):
            wname = "WBT" if which == "beta" else "WGKT"
            with nc.Block() as block:
                @block.gpsimd
                def _(g):
                    src = t[wname][:, :].rearrange("(n p) h -> p n h", p=128)
                    dst = wsb[:, 0:ND * H].rearrange("p (n h) -> p n h", n=ND)
                    self.s_gld.inc(g.dma_start(dst, src), 16)
                    if which == "beta":
                        self.s_gld.inc(g.dma_start(bcol[:], t["BB"][:]), 16)
                    else:
                        self.s_gld.inc(g.dma_start(bcol[:], t["BGK"][:]), 16)
                        self.s_gld.inc(g.dma_start(bcol2[:], t["DTB"][:]), 16)
                        self.s_gld.inc(g.dma_start(acol[:], t["ALOG"][:]), 16)

            for cchunk in range(4):
                with nc.Block() as block:
                    @block.tensor
                    def _(te):
                        s_ld.wait(te); self.s_gld.wait(te)
                        ins = None
                        for i in range(ND):
                            ins = te.matmul(
                                psz[0:H, :],
                                wsb[:, H * i:H * (i + 1)],
                                self.xn_fm[:, 2048 * i + 512 * cchunk:2048 * i + 512 * (cchunk + 1)],
                                start=(i == 0), stop=(i == ND - 1))
                        s_pe.inc(ins)

                    @block.vector
                    def _(v):
                        s_pe.wait(v)
                        s_v.inc(v.tensor_copy(zf[0:H, 512 * cchunk:512 * (cchunk + 1)], psz[0:H, :]))

            if which == "beta":
                with nc.Block() as block:
                    @block.scalar
                    def _(sc):
                        s_ld.wait(sc); self.s_gld.wait(sc)
                        s_sc.inc(sc.activation(out_fm[0:H, 0:T], zf[0:H, 0:T], AF.Sigmoid, bias=bcol[:]))

                    @block.sync
                    def _(sy):
                        s_sc.wait(sy)
                        s_st.inc(sy.dma_start(t["BETAFM"][s], out_fm[0:H, 0:T]), 16)
            else:
                with nc.Block() as block:
                    @block.vector
                    def _(v):
                        s_ld.wait(v); self.s_gld.wait(v)
                        s_v.inc(v.tensor_scalar(zf[0:H, 0:T], zf[0:H, 0:T], bcol[:], bcol2[:],
                                                ALU.add, ALU.add))

                    @block.scalar
                    def _(sc):
                        s_v.wait(sc)
                        sc.activation(zab[0:H, 0:T], zf[0:H, 0:T], AF.Abs)
                        sc.activation(zab[0:H, 0:T], zab[0:H, 0:T], AF.Exp, scale=-1.0)
                        s_sc.inc(sc.activation(zab[0:H, 0:T], zab[0:H, 0:T], AF.Ln, bias=1.0))
                        s_sc.inc(sc.activation(nea[:], acol[:], AF.Exp))

                    @block.vector
                    def _(v):
                        s_sc.wait(v)
                        v.tensor_relu(sp[0:H, 0:T], zf[0:H, 0:T])
                        v.tensor_add(sp[0:H, 0:T], sp[0:H, 0:T], zab[0:H, 0:T])
                        s_v.inc(v.tensor_scalar_mul(nea[:], nea[:], -1.0))
                        s_v.wait(v)
                        s_v.inc(v.tensor_scalar_mul(out_fm[0:H, 0:T], sp[0:H, 0:T], nea[:]))
                        v.memset(onesrow[:], 1.0)
                        s_v.wait(v)
                        ins = None
                        for c in range(CH):
                            ins = v.tensor_tensor_scan(
                                zab[0:H, C * c:C * (c + 1)], onesrow[:],
                                out_fm[0:H, C * c:C * (c + 1)], 0.0, ALU.mult, ALU.add)
                        s_v.inc(ins)

                    @block.sync
                    def _(sy):
                        s_v.wait(sy)
                        s_st.inc(sy.dma_start(t["CUMFM"][s], zab[0:H, 0:T]), 16)

    def vproj(self, s):
        p, nc = self.p, self.p.nc
        t = p.t
        s_ld, s_v, s_sc, s_pe, s_st = self.s_ld, self.s_v, self.s_sc, self.s_pe, self.s_st
        vpre = self.scr[0]
        acc = self.scr[1]
        vsil = p.sb("a_bfT1", (128, T), BF16)
        taps = p.sb("a_taps", (128, 4), F32)
        vbias = p.sb("a_bias", (128, 1), F32)
        psv = p.ps("PSA", (128, 512), F32)

        for i in range(ND):
            with nc.Block() as block:
                @block.gpsimd
                def _(g):
                    src = t["WVT"][:, 128 * i:128 * (i + 1)].rearrange("(n p) m -> p n m", p=128)
                    dst = self.wbuf[:, 0:ND * 128].rearrange("p (n m) -> p n m", n=ND)
                    self.s_gld.inc(g.dma_start(dst, src), 16)
                    self.s_gld.inc(g.dma_start(taps[:], t["VCONV"][128 * i:128 * (i + 1), :]), 16)
                    self.s_gld.inc(g.dma_start(vbias[:], t["VCB"][128 * i:128 * (i + 1), :]), 16)

                @block.vector
                def _(v):
                    s_v.inc(v.memset(vpre[:, 0:3], 0.0))

            for cc in range(4):
                with nc.Block() as block:
                    @block.tensor
                    def _(te):
                        s_ld.wait(te); self.s_gld.wait(te)
                        ins = None
                        for j in range(ND):
                            ins = te.matmul(
                                psv[:],
                                self.wbuf[:, 128 * j:128 * (j + 1)],
                                self.xn_fm[:, 2048 * j + 512 * cc:2048 * j + 512 * (cc + 1)],
                                start=(j == 0), stop=(j == ND - 1))
                        s_pe.inc(ins)

                    @block.vector
                    def _(v):
                        s_pe.wait(v)
                        s_v.inc(v.tensor_copy(vpre[:, 3 + 512 * cc:3 + 512 * (cc + 1)], psv[:]))

            with nc.Block() as block:
                @block.vector
                def _(v):
                    v.tensor_scalar_mul(acc[:, 0:T], vpre[:, 0:T], taps[:, 0:1])
                    for j in range(1, 4):
                        v.scalar_tensor_tensor(acc[:, 0:T], vpre[:, j:j + T], taps[:, j:j + 1],
                                               acc[:, 0:T], ALU.mult, ALU.add)
                    s_v.inc(v.engine_nop())

                @block.scalar
                def _(sc):
                    s_v.wait(sc)
                    s_st.wait(sc)
                    s_sc.inc(sc.activation(vsil[:], acc[:, 0:T], AF.Silu, bias=vbias[:]))

                @block.sync
                def _(sy):
                    s_sc.wait(sy)
                    s_st.inc(sy.dma_start(t["VFM"][s, 128 * i:128 * (i + 1), :], vsil[:]), 16)

    def gate(self, s):
        p, nc = self.p, self.p.nc
        t = p.t
        s_ld, s_v, s_pe, s_st = self.s_ld, self.s_v, self.s_pe, self.s_st
        gstage = p.sb("a_bf512", (128, 512), BF16)
        psg = p.ps("PSA", (128, 512), F32)

        for dc in range(8):
            with nc.Block() as block:
                @block.gpsimd
                def _(g):
                    src = t["WGT"][:, 256 * dc:256 * (dc + 1)].rearrange("(n p) m -> p n m", p=128)
                    dst = self.wbuf[:, 0:ND * 256].rearrange("p (n m) -> p n m", n=ND)
                    self.s_gld.inc(g.dma_start(dst, src), 16)

            for tt in range(NT):
                with nc.Block() as block:
                    @block.tensor
                    def _(te):
                        s_ld.wait(te); self.s_gld.wait(te)
                        ins = None
                        for i in range(ND):
                            ins = te.matmul(
                                psg[:, 0:256],
                                self.xn_fm[:, 2048 * i + 128 * tt:2048 * i + 128 * (tt + 1)],
                                self.wbuf[:, 256 * i:256 * (i + 1)],
                                start=(i == 0), stop=(i == ND - 1))
                        s_pe.inc(ins)

                    @block.vector
                    def _(v):
                        s_pe.wait(v)
                        s_st.wait(v, lag=16)
                        s_v.inc(v.tensor_copy(gstage[:, 0:256], psg[:, 0:256]))

                    @block.sync
                    def _(sy):
                        s_v.wait(sy)
                        s_st.inc(sy.dma_start(
                            t["GT"][s, 128 * tt:128 * (tt + 1), 256 * dc:256 * (dc + 1)],
                            gstage[:, 0:256]), 16)

    def qk(self, s, which):
        p, nc = self.p, self.p.nc
        t = p.t
        s_ld, s_v, s_sc, s_pe, s_st, s_g = (self.s_ld, self.s_v, self.s_sc,
                                            self.s_pe, self.s_st, self.s_g)
        wname = "WQT" if which == "q" else "WKT"
        cname = "QCONV" if which == "q" else "KCONV"
        bname = "QCB" if which == "q" else "KCB"
        oname = "QF" if which == "q" else "KF"

        pre_e, pre_o = self.scr[0], self.scr[1]
        sil_e, sil_o = self.scr[2], self.scr[3]
        rot_e, rot_o = self.scr[4], self.scr[5]
        sq_e = p.sb("a_sq_e", (128, 512), F32)
        sq_o = p.sb("a_sq_o", (128, 512), F32)
        rsb = p.sb("a_rs", (4, 512), F32)
        fin_e = p.sb("a_bfT1", (128, T), BF16)
        fin_o = p.sb("a_bfT2", (128, T), BF16)
        taps_e = p.sb("a_taps", (128, 4), F32)
        taps_o = p.sb("a_taps2", (128, 4), F32)
        bias_e = p.sb("a_bias", (128, 1), F32)
        bias_o = p.sb("a_bias2", (128, 1), F32)
        psq = p.ps("PSA", (128, 512), F32)
        psl2 = p.ps("PSB", (128, 512), F32)
        psbc = p.ps("PSC", (128, 512), F32)

        for i in range(8):
            io = i + 8
            with nc.Block() as block:
                @block.gpsimd
                def _(g):
                    src_e = t[wname][:, 128 * i:128 * (i + 1)].rearrange("(n p) m -> p n m", p=128)
                    src_o = t[wname][:, 128 * io:128 * (io + 1)].rearrange("(n p) m -> p n m", p=128)
                    dst_e = self.wbuf[:, 0:ND * 128].rearrange("p (n m) -> p n m", n=ND)
                    dst_o = self.wbuf[:, ND * 128:2 * ND * 128].rearrange("p (n m) -> p n m", n=ND)
                    self.s_gld.inc(g.dma_start(dst_e, src_e), 16)
                    self.s_gld.inc(g.dma_start(dst_o, src_o), 16)
                    self.s_gld.inc(g.dma_start(taps_e[:], t[cname][128 * i:128 * (i + 1), :]), 16)
                    self.s_gld.inc(g.dma_start(taps_o[:], t[cname][128 * io:128 * (io + 1), :]), 16)
                    self.s_gld.inc(g.dma_start(bias_e[:], t[bname][128 * i:128 * (i + 1), :]), 16)
                    self.s_gld.inc(g.dma_start(bias_o[:], t[bname][128 * io:128 * (io + 1), :]), 16)

                @block.vector
                def _(v):
                    s_st.wait(v)
                    v.memset(pre_e[:, 0:3], 0.0)
                    s_v.inc(v.memset(pre_o[:, 0:3], 0.0))

            for pre, woff in ((pre_e, 0), (pre_o, ND * 128)):
                for cc in range(4):
                    with nc.Block() as block:
                        @block.tensor
                        def _(te):
                            s_ld.wait(te); self.s_gld.wait(te)
                            ins = None
                            for j in range(ND):
                                ins = te.matmul(
                                    psq[:],
                                    self.wbuf[:, woff + 128 * j:woff + 128 * (j + 1)],
                                    self.xn_fm[:, 2048 * j + 512 * cc:2048 * j + 512 * (cc + 1)],
                                    start=(j == 0), stop=(j == ND - 1))
                            s_pe.inc(ins)

                        @block.vector
                        def _(v):
                            s_pe.wait(v)
                            s_v.inc(v.tensor_copy(pre[:, 3 + 512 * cc:3 + 512 * (cc + 1)], psq[:]))

            for pre, sil, taps, bias in ((pre_e, sil_e, taps_e, bias_e),
                                         (pre_o, sil_o, taps_o, bias_o)):
                with nc.Block() as block:
                    @block.vector
                    def _(v):
                        v.tensor_scalar_mul(rot_e[:, 0:T], pre[:, 0:T], taps[:, 0:1])
                        for j in range(1, 4):
                            v.scalar_tensor_tensor(rot_e[:, 0:T], pre[:, j:j + T], taps[:, j:j + 1],
                                                   rot_e[:, 0:T], ALU.mult, ALU.add)
                        s_v.inc(v.engine_nop())

                    @block.scalar
                    def _(sc):
                        s_v.wait(sc)
                        s_sc.inc(sc.activation(sil[:, 0:T], rot_e[:, 0:T], AF.Silu, bias=bias[:]))

            with nc.Block() as block:
                @block.vector
                def _(v):
                    s_sc.wait(v)
                    v.tensor_mul(pre_e[:, 0:T], sil_e[:, 0:T], self.cosb[:])
                    v.tensor_mul(pre_o[:, 0:T], sil_o[:, 0:T], self.sinb[:])
                    s_v.inc(v.tensor_sub(rot_e[:, 0:T], pre_e[:, 0:T], pre_o[:, 0:T]))

                @block.gpsimd
                def _(g):
                    s_sc.wait(g)
                    g.tensor_mul(rot_o[:, 0:T], sil_e[:, 0:T], self.sinb[:])
                    s_g.inc(g.engine_nop())

            with nc.Block() as block:
                @block.vector
                def _(v):
                    v.tensor_mul(pre_e[:, 0:T], sil_o[:, 0:T], self.cosb[:])
                    s_v.inc(v.tensor_add(rot_o[:, 0:T], rot_o[:, 0:T], pre_e[:, 0:T]))

            for cc in range(4):
                with nc.Block() as block:
                    @block.scalar
                    def _(sc):
                        sc.activation(sq_e[:], rot_e[:, 512 * cc:512 * (cc + 1)], AF.Square)
                        s_sc.inc(sc.activation(sq_o[:], rot_o[:, 512 * cc:512 * (cc + 1)], AF.Square))

                    @block.tensor
                    def _(te):
                        s_sc.wait(te)
                        te.matmul(psl2[0:4, :], self.ones32[:], sq_e[:], start=True, stop=False)
                        s_pe.inc(te.matmul(psl2[0:4, :], self.ones32[:], sq_o[:], start=False, stop=True))

                with nc.Block() as block:
                    @block.vector
                    def _(v):
                        s_pe.wait(v)
                        s_v.inc(v.tensor_scalar_add(rsb[:], psl2[0:4, :], EPS_L2))
                        s_v.wait(v)
                        s_v.inc(v.reciprocal(rsb[:], rsb[:]))

                    @block.scalar
                    def _(sc):
                        s_v.wait(sc)
                        s_sc.inc(sc.activation(rsb[:], rsb[:], AF.Sqrt))

                with nc.Block() as block:
                    @block.tensor
                    def _(te):
                        s_sc.wait(te)
                        s_pe.inc(te.matmul(psbc[:], self.sel32[:], rsb[:], start=True, stop=True))

                    @block.vector
                    def _(v):
                        s_pe.wait(v)
                        v.tensor_mul(fin_e[:, 512 * cc:512 * (cc + 1)],
                                     rot_e[:, 512 * cc:512 * (cc + 1)], psbc[:])
                        s_v.inc(v.tensor_mul(fin_o[:, 512 * cc:512 * (cc + 1)],
                                             rot_o[:, 512 * cc:512 * (cc + 1)], psbc[:]))

            with nc.Block() as block:
                @block.sync
                def _(sy):
                    s_v.wait(sy)
                    for u in range(4):
                        hh = 4 * i + u
                        s_st.inc(sy.dma_start(t[oname][s, hh, 0:32, :],
                                              fin_e[32 * u:32 * (u + 1), :]), 16)
                        s_st.inc(sy.dma_start(t[oname][s, hh, 32:64, :],
                                              fin_o[32 * u:32 * (u + 1), :]), 16)

    def tok_pass(self, s, which):
        p, nc = self.p, self.p.nc
        t = p.t
        s_ld, s_st = self.s_ld, self.s_st
        ktt = p.sb("a_bfT1", (128, T), BF16)

        for j in range(NPAIR):
            with nc.Block() as block:
                @block.sync
                def _(sy):
                    s_st.wait(sy)
                    for tt in range(NT):
                        if which == "k":
                            src = t["KF"][s, 2 * j:2 * j + 2].rearrange(
                                "h d tc -> (h d) tc")[:, 128 * tt:128 * (tt + 1)]
                        else:
                            src = t["VFM"][s, 128 * j:128 * (j + 1), 128 * tt:128 * (tt + 1)]
                        s_ld.inc(sy.dma_start(ktt[:, 128 * tt:128 * (tt + 1)],
                                              src, transpose=True), 16)

            with nc.Block() as block:
                @block.sync
                def _(sy):
                    s_ld.wait(sy); self.s_gld.wait(sy)
                    dname = "KT_PAIR" if which == "k" else "VT_PAIR"
                    for half in range(2):
                        for hh in range(2):
                            src = ktt[64 * half:64 * (half + 1), :].rearrange(
                                "pp (tc hh k) -> pp tc hh k", tc=NT, hh=2)[:, :, hh]
                            dst = t[dname][s, half::2, j][:, 64 * hh:64 * (hh + 1), :].rearrange(
                                "c pp k -> pp c k")
                            s_st.inc(sy.dma_start(dst, src), 16)
                    if which == "v":
                        dst = t["VT"][s, :, 128 * j:128 * (j + 1)].rearrange(
                            "(tc pp) c2 -> pp tc c2", pp=128)
                        src = ktt[:, :].rearrange("pp (tc c2) -> pp tc c2", tc=NT)
                        s_st.inc(sy.dma_start(dst, src), 16)
                    s_st.wait(sy)


def emit_stageA_all(p):
    sa = StageA(p)
    sa.load_consts()
    for s in range(NSEQ):
        sa.ln(s)
        sa.beta_g(s)
        sa.vproj(s)
        sa.gate(s)
        sa.qk(s, "q")
        sa.qk(s, "k")
        sa.tok_pass(s, "k")
        sa.tok_pass(s, "v")
    return sa




def _scan_masks():
    pm = np.arange(128) % 64
    l = np.arange(64)
    maskL = (l[None, :] < pm[:, None]).astype(np.float32)
    maskUS = (pm[:, None] < l[None, :]).astype(np.float32)
    maskUI = (pm[:, None] <= l[None, :]).astype(np.float32)
    return maskL, maskUS, maskUI


def prep_stageB():
    maskL, maskUS, maskUI = _scan_masks()
    sel2 = np.zeros((2, 128), dtype=np.float32)
    sel2[0, 0:64] = 1.0
    sel2[1, 64:128] = 1.0
    blkid = np.zeros((16, 1024), dtype=np.float32)
    for i in range(16):
        blkid[i, 64 * i:64 * (i + 1)] = 1.0
    return {"MASKL": maskL, "MASKUS": maskUS, "MASKUI": maskUI, "SEL2": sel2,
            "BLKID16": blkid}


STAGEB_IN = ["QF", "KF", "KT_PAIR", "VT_PAIR", "BETAFM", "CUMFM",
             "MASKL", "MASKUS", "MASKUI", "SEL2", "BLKID16", "IDENTF"]
STAGEB_OUT = ["OT"]


def declare_stageB(p, out, shared_with_A=False):
    if not shared_with_A:
        p.din("QF", (NSEQ, H, DH, T), BF16)
        p.din("KF", (NSEQ, H, DH, T), BF16)
        p.din("KT_PAIR", (NSEQ, CH, NPAIR, 128, DH), BF16)
        p.din("VT_PAIR", (NSEQ, CH, NPAIR, 128, DH), BF16)
        p.din("BETAFM", (NSEQ, H, T), F32)
        p.din("CUMFM", (NSEQ, H, T), F32)
        p.din("IDENTF", (128, 128), F32)
    p.din("MASKL", (128, C), F32)
    p.din("MASKUS", (128, C), F32)
    p.din("MASKUI", (128, C), F32)
    p.din("SEL2", (2, 128), F32)
    p.din("BLKID16", (16, 1024), F32)
    out("OT", (NSEQ, CH, NPAIR, 128, DH), BF16)


def bc_ap(buf, col0, ncols, rep):
    rowlen = buf.shape[1]
    return bass.AP(buf, col0, [[rowlen, buf.shape[0]], [1, ncols], [0, rep]])


def blk_ap(buf, nblk):
    rowlen = buf.shape[1]
    return bass.AP(buf, 0, [[rowlen, buf.shape[0]], [0, nblk], [1, rowlen]])


class StageB:
    def __init__(self, p):
        self.p = p
        self.maskL = p.sb("b_maskL", (128, C), F32)
        self.maskUS = p.sb("b_maskUS", (128, C), F32)
        self.maskUI = p.sb("b_maskUI", (128, C), F32)
        self.sel2 = p.sb("b_sel2", (2, 128), F32)
        self.blkid = p.sb("b_blkid", (16, 1024), F32)
        self.identf = p.sb("b_idf", (128, 128), F32)
        self.kfc = p.sb("b_kfc", (128, 1024), BF16)
        self.qfc = p.sb("b_qfc", (128, 1024), BF16)
        self.ktc = p.sb("b_ktc", (128, 1024), BF16)
        self.vtc = p.sb("b_vtc", (128, 1024), BF16)
        self.c2c = p.sb("b_c2c", (2, 1024), F32)
        self.b2c = p.sb("b_b2c", (2, 1024), F32)
        self.cdg = p.sb("b_cdg", (16, 128), F32)
        self.negcd = p.sb("b_negcd", (16, 128), F32)
        self.ends = p.sb("b_ends", (2, 16), F32)
        self.ccol = p.sb("b_ccol", (128, 16), F32)
        self.bcol = p.sb("b_bcol", (128, 16), F32)
        self.nbcol = p.sb("b_nbcol", (128, 16), F32)
        self.endb = p.sb("b_endb", (128, 16), F32)
        self.expccc = p.sb("b_expccc", (128, 16), F32)
        self.gendc = p.sb("b_gendc", (128, 16), F32)
        self.ecnc = p.sb("b_ecnc", (128, 16), F32)
        self.nbgc = p.sb("b_nbgc", (128, 16), F32)
        scr = [p.sb("scr%d" % i, (128, 2052), F32) for i in range(6)]
        self.t1 = scr[0][:, 0:1024]
        self.expd = scr[0][:, 1024:2048]
        self.abuf = scr[1][:, 0:1024]
        self.bbuf = scr[1][:, 1024:2048]
        self.bfree = scr[2][:, 0:1024]
        self.ubuf = scr[2][:, 1024:2048]
        self.pwA = [scr[3][:, 0:1024], scr[3][:, 1024:2048]]
        self.pwB = [scr[4][:, 0:1024], scr[4][:, 1024:2048],
                    scr[5][:, 0:1024], scr[5][:, 1024:2048],
                    p.sb("b_pwB4", (128, 1024), F32)]
        self.att = p.sb("b_att", (128, 1024), BF16)
        self.ubf = p.sb("b_ubf", (128, 1024), BF16)
        self.kb = p.sb("b_kb", (128, 1024), BF16)
        self.s_all = p.sb("b_s", (128, 1024), F32)
        self.s_bf = p.sb("b_sbf", (128, 1024), BF16)
        self.ostage = p.sb("b_ostage", (128, 1024), BF16)
        self.ps = {n: p.ps(n, (128, 512), F32) for n in
                   ("PSA", "PSB", "PSC", "PSD", "PSE", "PSF")}
        self.s_ld = p.sem("b_ld")
        self.s_gld = p.sem("b_gld")
        self.s_v = p.sem("b_v")
        self.s_sc = p.sem("b_sc")
        self.s_pe = p.sem("b_pe")
        self.s_st = p.sem("b_st")
        self.s_g = p.sem("b_g")

    def load_consts(self):
        p, nc = self.p, self.p.nc
        with nc.Block() as block:
            @block.sync
            def _(sy):
                for dst, src in ((self.maskL, "MASKL"), (self.maskUS, "MASKUS"),
                                 (self.maskUI, "MASKUI"), (self.sel2, "SEL2"),
                                 (self.blkid, "BLKID16"), (self.identf, "IDENTF")):
                    self.s_ld.inc(sy.dma_start(dst[:], p.t[src][:]), 16)

    def seq_init(self, s):
        p, nc = self.p, self.p.nc
        with nc.Block() as block:
            @block.vector
            def _(v):
                v.memset(self.s_all[:], 0.0)
                self.s_v.inc(v.memset(self.s_bf[:], 0.0))

    def _mm_pairs(self, te, psL, psR, lhs, rhs):
        ins = None
        for i in range(NPAIR):
            ps = psL if i < 8 else psR
            o = 64 * (i % 8)
            for hh in range(2):
                r0 = 64 * hh
                ins = te.matmul(ps[r0:r0 + 64, o:o + 64],
                                lhs[r0:r0 + 64, 64 * i:64 * (i + 1)],
                                rhs[r0:r0 + 64, 64 * i:64 * (i + 1)],
                                start=True, stop=True, tile_position=(r0, r0))
        return ins

    PHASES = 99

    def chunk(self, s, c):
        p, nc = self.p, self.p.nc
        t = p.t
        s_ld, s_v, s_sc, s_pe, s_st, s_g = (self.s_ld, self.s_v, self.s_sc,
                                            self.s_pe, self.s_st, self.s_g)
        PSA, PSB, PSC, PSD, PSE, PSF = (self.ps[n] for n in
                                        ("PSA", "PSB", "PSC", "PSD", "PSE", "PSF"))
        r3 = lambda buf: buf[:].rearrange("p (i l) -> p i l", i=NPAIR)
        r3h = lambda ps: ps[:].rearrange("p (i l) -> p i l", i=8)

        with nc.Block() as block:
            @block.sync
            def _(sy):
                for dst, srcn in ((self.kfc, "KF"), (self.qfc, "QF")):
                    for hh in range(2):
                        sap = t[srcn][s][hh::2, :, C * c:C * (c + 1)].rearrange("i d j -> d i j")
                        dap = dst[64 * hh:64 * (hh + 1), :].rearrange("d (i j) -> d i j", i=NPAIR)
                        s_ld.inc(sy.dma_start(dap, sap), 16)
                sap = t["KT_PAIR"][s, c].rearrange("i pp k -> pp i k")
                s_ld.inc(sy.dma_start(r3(self.ktc), sap), 16)
                sap = t["VT_PAIR"][s, c].rearrange("i pp k -> pp i k")
                s_ld.inc(sy.dma_start(r3(self.vtc), sap), 16)
                for dst2, src2 in ((self.c2c, "CUMFM"), (self.b2c, "BETAFM")):
                    sap = t[src2][s][:, C * c:C * (c + 1)].rearrange("(i hh) l -> hh i l", hh=2)
                    dap = dst2[:].rearrange("hh (i l) -> hh i l", i=NPAIR)
                    s_ld.inc(sy.dma_start(dap, sap), 16)
                sap = t["CUMFM"][s][:, C * c:C * (c + 1)].rearrange("(i hh) pp -> i hh pp", hh=2)
                dap = self.cdg[:].rearrange("i (hh pp) -> i hh pp", hh=2)
                s_ld.inc(sy.dma_start(dap, sap), 16)
                sap = t["BETAFM"][s][:, C * c:C * (c + 1)].rearrange("(i hh) pp -> i hh pp", hh=2)
                dap = self.negcd[:].rearrange("i (hh pp) -> i hh pp", hh=2)
                s_ld.inc(sy.dma_start(dap, sap), 16)
                sap = t["CUMFM"][s][:, C * c + 63:C * c + 64].rearrange("(i hh) one -> hh i one", hh=2)
                dap = self.ends[:].rearrange("hh (i one) -> hh i one", one=1)
                s_ld.inc(sy.dma_start(dap, sap, allow_slow_non_contiguous=True), 16)

        if self.PHASES < 1:
            return
        with nc.Block() as block:
            @block.tensor
            def _(te):
                s_ld.wait(te); self.s_gld.wait(te)
                te.transpose(PSE[:, 0:16], self.cdg[:], self.identf[0:16, 0:16])
                te.transpose(PSE[:, 16:32], self.negcd[:], self.identf[0:16, 0:16])
                s_pe.inc(te.matmul(PSE[:, 32:48], self.sel2[:], self.ends[:],
                                   start=True, stop=True))

            @block.vector
            def _(v):
                s_pe.wait(v)
                v.tensor_copy(self.ccol[:], PSE[:, 0:16])
                v.tensor_copy(self.bcol[:], PSE[:, 16:32])
                v.tensor_scalar_mul(self.nbcol[:], PSE[:, 16:32], -1.0)
                v.tensor_copy(self.endb[:], PSE[:, 32:48])
                s_v.inc(v.tensor_scalar_mul(self.negcd[:], self.cdg[:], -1.0))
                s_v.wait(v)
                s_v.inc(v.tensor_sub(self.ecnc[:], self.endb[:], self.ccol[:]))

            @block.scalar
            def _(sc):
                s_v.wait(sc)
                sc.activation(self.expccc[:], self.ccol[:], AF.Exp)
                sc.activation(self.gendc[:], self.endb[:], AF.Exp)
                s_sc.inc(sc.activation(self.ecnc[:], self.ecnc[:], AF.Exp))

            @block.vector
            def _(v):
                s_sc.wait(v)
                s_v.inc(v.tensor_mul(self.nbgc[:], self.nbcol[:], self.expccc[:]))

        if self.PHASES < 2:
            return
        nbetcb = bc_ap(self.nbcol, 0, 16, C)
        betcb = bc_ap(self.bcol, 0, 16, C)
        ecnb = bc_ap(self.ecnc, 0, 16, C)
        gendb = bc_ap(self.gendc, 0, 16, C)
        nbgb = lambda half: bc_ap(self.nbgc, 8 * half, 8, C)
        expccb = lambda half: bc_ap(self.expccc, 8 * half, 8, C)

        with nc.Block() as block:
            @block.tensor
            def _(te):
                self._mm_pairs(te, PSA, PSB, self.kfc, self.kfc)
                self._mm_pairs(te, PSC, PSD, self.kfc, self.qfc)
                for half, ps in ((0, PSE), (1, PSF)):
                    te.matmul(ps[:], self.sel2[:],
                              self.c2c[:, 512 * half:512 * (half + 1)],
                              start=True, stop=False)
                    ins = te.matmul(ps[:], self.negcd[:],
                                    self.blkid[:, 512 * half:512 * (half + 1)],
                                    start=False, stop=True)
                s_pe.inc(ins)

        if self.PHASES < 3:
            return
        with nc.Block() as block:
            @block.vector
            def _(v):
                s_pe.wait(v)
                for half, ps in ((0, PSE), (1, PSF)):
                    sl = slice(512 * half, 512 * (half + 1))
                    ins = v.tensor_scalar(self.t1[:, sl], ps[:], -60.0, 60.0,
                                          ALU.max, ALU.min)
                s_v.inc(ins)

            @block.scalar
            def _(sc):
                s_v.wait(sc)
                sc.activation(self.expd[:], self.t1[:], AF.Exp, scale=-1.0)
                s_sc.inc(sc.activation(self.t1[:], self.t1[:], AF.Exp))

        if self.PHASES < 4:
            return
        with nc.Block() as block:
            @block.tensor
            def _(te):
                for half, ps in ((0, PSE), (1, PSF)):
                    ins = te.matmul(ps[:], self.sel2[:],
                                    self.b2c[:, 512 * half:512 * (half + 1)],
                                    start=True, stop=True)
                s_pe.inc(ins)

            @block.vector
            def _(v):
                s_pe.wait(v)
                for half, ps in ((0, PSE), (1, PSF)):
                    ins = v.tensor_scalar_mul(self.bfree[:, 512 * half:512 * (half + 1)],
                                              ps[:], -1.0)
                s_v.inc(ins)

        if self.PHASES < 5:
            return
        with nc.Block() as block:
            @block.vector
            def _(v):
                s_sc.wait(v)
                for half, ps in ((0, PSA), (1, PSB)):
                    sl = slice(512 * half, 512 * (half + 1))
                    v.tensor_tensor(self.abuf[:, sl], self.expd[:, sl], ps[:], ALU.mult)
                for half, ps in ((0, PSA), (1, PSB)):
                    sl = slice(512 * half, 512 * (half + 1))
                    v.tensor_tensor(self.expd[:, sl], self.t1[:, sl], ps[:], ALU.mult)
                for half, ps in ((0, PSC), (1, PSD)):
                    sl = slice(512 * half, 512 * (half + 1))
                    v.tensor_tensor(self.t1[:, sl], self.t1[:, sl], ps[:], ALU.mult)
                s_v.inc(v.engine_nop())

            @block.gpsimd
            def _(g):
                s_v.wait(g)
                g.tensor_tensor(r3(self.abuf), r3(self.abuf), nbetcb, ALU.mult)
                g.tensor_tensor(r3(self.abuf), r3(self.abuf), blk_ap(self.maskL, NPAIR), ALU.mult)
                g.tensor_tensor(self.bbuf[:], self.expd[:], self.bfree[:], ALU.mult)
                g.tensor_tensor(r3(self.bbuf), r3(self.bbuf), blk_ap(self.maskUS, NPAIR), ALU.mult)
                s_g.inc(g.tensor_tensor(r3(self.att), r3(self.t1), blk_ap(self.maskUI, NPAIR), ALU.mult))

        if self.PHASES < 6:
            return
        aprev, bprev = self.abuf, self.bbuf
        for r in range(5):
            anew = self.pwA[r % 2]
            bnew = self.pwB[r]
            with nc.Block() as block:
                @block.tensor
                def _(te):
                    if r == 0:
                        s_g.wait(te)
                    if r < 4:
                        self._mm_pairs(te, PSA, PSB, bprev, aprev)
                    s_pe.inc(self._mm_pairs(te, PSC, PSD, aprev, bprev))

                @block.vector
                def _(v):
                    s_pe.wait(v)
                    if r < 4:
                        v.tensor_copy(anew[:, 0:512], PSA[:])
                        v.tensor_copy(anew[:, 512:1024], PSB[:])
                    v.tensor_copy(bnew[:, 0:512], PSC[:])
                    s_v.inc(v.tensor_copy(bnew[:, 512:1024], PSD[:]))
            aprev, bprev = anew, bnew

        if self.PHASES < 7:
            return
        with nc.Block() as block:
            @block.tensor
            def _(te):
                s_pe.inc(self._mm_pairs(te, PSE, PSF, self.kfc, self.s_bf))

            @block.vector
            def _(v):
                s_pe.wait(v)
                v.tensor_tensor(r3(self.ubuf), r3(self.vtc), betcb, ALU.mult)
                for half, ps in ((0, PSE), (1, PSF)):
                    sl = slice(512 * half, 512 * (half + 1))
                    v.tensor_tensor(self.expd[:, sl].rearrange("p (i l) -> p i l", i=8),
                                    r3h(ps), nbgb(half), ALU.mult)
                s_v.inc(v.tensor_add(self.ubuf[:], self.ubuf[:], self.expd[:]))

        if self.PHASES < 8:
            return
        powers = [self.bbuf] + self.pwB
        for k2 in range(6):
            psL, psR = (PSA, PSB) if k2 % 2 == 0 else (PSC, PSD)
            with nc.Block() as block:
                @block.tensor
                def _(te):
                    s_pe.inc(self._mm_pairs(te, psL, psR, powers[k2], self.ubuf))

                @block.vector
                def _(v):
                    s_pe.wait(v)
                    v.tensor_add(self.ubuf[:, 0:512], self.ubuf[:, 0:512], psL[:])
                    s_v.inc(v.tensor_add(self.ubuf[:, 512:1024], self.ubuf[:, 512:1024], psR[:]))

        if self.PHASES < 9:
            return
        with nc.Block() as block:
            @block.vector
            def _(v):
                v.tensor_tensor(r3(self.kb), r3(self.ktc), ecnb, ALU.mult)
                s_v.inc(v.tensor_copy(self.ubf[:], self.ubuf[:]))

        if self.PHASES < 10:
            return
        with nc.Block() as block:
            @block.tensor
            def _(te):
                self._mm_pairs(te, PSA, PSB, self.qfc, self.s_bf)
                self._mm_pairs(te, PSC, PSD, self.kb, self.ubf)
                s_pe.inc(self._mm_pairs(te, PSE, PSF, self.att, self.ubf))

            @block.vector
            def _(v):
                s_pe.wait(v)
                v.tensor_tensor(r3(self.s_all), r3(self.s_all), gendb, ALU.mult)
                v.tensor_add(self.s_all[:, 0:512], self.s_all[:, 0:512], PSC[:])
                v.tensor_add(self.s_all[:, 512:1024], self.s_all[:, 512:1024], PSD[:])
                s_v.inc(v.tensor_copy(self.s_bf[:], self.s_all[:]))
                for half, ps in ((0, PSA), (1, PSB)):
                    sl = slice(512 * half, 512 * (half + 1))
                    v.tensor_tensor(self.expd[:, sl].rearrange("p (i l) -> p i l", i=8),
                                    r3h(ps), expccb(half), ALU.mult)
                s_st.wait(v, lag=16)
                v.tensor_add(self.ostage[:, 0:512], self.expd[:, 0:512], PSE[:])
                s_v.inc(v.tensor_add(self.ostage[:, 512:1024], self.expd[:, 512:1024], PSF[:]))

            @block.sync
            def _(sy):
                s_v.wait(sy)
                dap = t["OT"][s, c].rearrange("i pp k -> pp i k")
                s_st.inc(sy.dma_start(dap, r3(self.ostage)), 16)


def emit_stageB_all(p, nseq=NSEQ, nch=CH):
    sb = StageB(p)
    sb.load_consts()
    for s in range(nseq):
        sb.seq_init(s)
        for c in range(nch):
            sb.chunk(s, c)
    with p.nc.Block() as block:
        @block.sync
        def _(sy):
            sb.s_st.wait(sy)
    return sb



def prep_stageB2(inputs):
    A = lambda k: np.asarray(inputs[k])
    d = {}
    d["WOT"] = np.ascontiguousarray(A("L0_attn_Wo").T).astype(np.float32)
    d["DROW"] = np.repeat(A("L0_attn_D").astype(np.float32), DH).reshape(1, D)
    d["ONROW"] = np.tile(A("L0_attn_onorm_w").astype(np.float32), H).reshape(1, D)
    d["ONES1"] = np.ones((1, 128), dtype=np.float32)
    return d


STAGEB2_IN = ["OT", "VT", "GT", "XN_TOK", "X", "WOT", "DROW", "ONROW", "ONES1",
              "IDENT"]
STAGEB2_OUT = ["X1"]


def declare_stageB2(p, out, shared=()):
    def din(name, shape, dtype=F32):
        if name not in p.t:
            p.din(name, shape, dtype)
    din("OT", (NSEQ, CH, NPAIR, 128, DH), BF16)
    din("VT", (NSEQ, T, D), BF16)
    din("GT", (NSEQ, T, D), BF16)
    din("XN_TOK", (NSEQ, T, D), F32)
    din("X", (NSEQ, T, D), F32)
    din("WOT", (D, D), F32)
    din("DROW", (1, D), F32)
    din("ONROW", (1, D), F32)
    din("ONES1", (1, 128), F32)
    din("IDENT", (128, 128), BF16)
    out("X1", (NSEQ, T, D), F32)


class StageB2:
    def __init__(self, p):
        self.p = p
        self.ofm = p.sb("a_xnfm", (128, ND * 2048), BF16)
        self.wbuf = p.sb("a_wbuf", (128, 4096), BF16)
        self.ident = p.sb("a_id", (128, 128), BF16)
        scr = [p.sb("scr%d" % i, (128, 2052), F32) for i in range(6)]
        self.db = scr[2][:, 0:D]
        self.onb = scr[3][:, 0:D]
        self.rowbuf = p.sb("rowbuf", (1, 512), F32)
        self.ones1 = p.sb("b2_ones1", (1, 128), F32)
        self.scr0h = scr[0]
        self.o1 = scr[0][:, 0:D]
        self.sq = scr[1][:, 0:D]
        self.vt = p.sb("b2_vt", (128, D), BF16)
        self.gt = p.sb("b2_gt", (128, D), BF16)
        self.o3 = p.sb("b2_o3", (128, D), BF16)
        self.ss = p.sb("b2_ss", (128, H), F32)
        self.xsl = p.sb("b2_xsl", (128, 512), F32)
        self.xsl2 = p.sb("b2_xsl2", (128, 512), F32)
        self.ps = {n: p.ps(n, (128, 512), F32) for n in ("PSA", "PSB")}
        self.pst = p.ps("PST", (128, 1024), BF16)
        self.s_ld = p.sem("b2_ld")
        self.s_gld = p.sem("b2_gld")
        self.s_v = p.sem("b2_v")
        self.s_sc = p.sem("b2_sc")
        self.s_pe = p.sem("b2_pe")
        self.s_st = p.sem("b2_st")

    def prologue(self):
        p, nc = self.p, self.p.nc
        t = p.t
        s_ld, s_v, s_pe = self.s_ld, self.s_v, self.s_pe
        with nc.Block() as block:
            @block.sync
            def _(sy):
                for dst, srcn in ((self.ones1, "ONES1"), (self.ident, "IDENT")):
                    s_ld.inc(sy.dma_start(dst[:], t[srcn][:]), 16)
        for dst, srcn in ((self.db, "DROW"), (self.onb, "ONROW")):
            for ccg in range(4):
                with nc.Block() as block:
                    @block.sync
                    def _(sy):
                        s_ld.inc(sy.dma_start(self.rowbuf[:],
                                              t[srcn][0:1, 512 * ccg:512 * (ccg + 1)]), 16)

                    @block.tensor
                    def _(te):
                        s_ld.wait(te); self.s_gld.wait(te)
                        s_pe.inc(te.matmul(self.ps["PSA"][:], self.ones1[:],
                                           self.rowbuf[:], start=True, stop=True))

                    @block.vector
                    def _(v):
                        s_pe.wait(v)
                        s_v.inc(v.tensor_copy(dst[:, 512 * ccg:512 * (ccg + 1)],
                                              self.ps["PSA"][:]))

    def opost(self, s):
        p, nc = self.p, self.p.nc
        t = p.t
        s_ld, s_v, s_sc, s_pe, s_st = (self.s_ld, self.s_v, self.s_sc,
                                       self.s_pe, self.s_st)
        for tt in range(NT):
            with nc.Block() as block:
                @block.gpsimd
                def _(g):
                    for half in range(2):
                        for hh in range(2):
                            sap = t["OT"][s, 2 * tt + half][:, 64 * hh:64 * (hh + 1), :] \
                                .rearrange("i pp v -> pp i v")
                            dap = bass.AP(
                                self.scr0h, 64 * half * 2052 + 64 * hh,
                                [[2052, 64], [128, NPAIR], [1, DH]])
                            self.s_gld.inc(g.dma_start(dap, sap), 16)

                @block.sync
                def _(sy):
                    s_ld.inc(sy.dma_start(self.vt[:], t["VT"][s, 128 * tt:128 * (tt + 1), :]), 16)
                    s_ld.inc(sy.dma_start(self.gt[:], t["GT"][s, 128 * tt:128 * (tt + 1), :]), 16)

                @block.vector
                def _(v):
                    s_ld.wait(v); self.s_gld.wait(v)
                    v.tensor_mul(self.sq[:], self.db[:], self.vt[:])
                    s_v.inc(v.tensor_add(self.o1[:], self.o1[:], self.sq[:]))

                @block.scalar
                def _(sc):
                    s_v.wait(sc)
                    s_sc.inc(sc.activation(self.sq[:], self.o1[:], AF.Square))

                @block.vector
                def _(v):
                    s_sc.wait(v)
                    s_v.inc(v.reduce_sum(self.ss[:], self.sq[:].rearrange(
                        "p (h dv) -> p h dv", h=H), axis=AX.X))
                    s_v.wait(v)
                    s_v.inc(v.tensor_scalar(self.ss[:], self.ss[:], 1.0 / DH, EPS_L2,
                                            ALU.mult, ALU.add))

                @block.scalar
                def _(sc):
                    s_v.wait(sc)
                    s_sc.inc(sc.activation(self.ss[:], self.ss[:], AF.Ln))
                    s_sc.wait(sc)
                    s_sc.inc(sc.activation(self.ss[:], self.ss[:], AF.Exp, scale=-0.5))

                @block.vector
                def _(v):
                    s_sc.wait(v)
                    rsb = bass.AP(self.ss, 0, [[self.ss.shape[1], 128], [1, H], [0, DH]])
                    v.tensor_tensor(self.o1[:].rearrange("p (h dv) -> p h dv", h=H),
                                    self.o1[:].rearrange("p (h dv) -> p h dv", h=H),
                                    rsb, ALU.mult)
                    s_v.inc(v.tensor_mul(self.o1[:], self.o1[:], self.onb[:]))

                @block.scalar
                def _(sc):
                    s_v.wait(sc)
                    s_sc.inc(sc.activation(self.sq[:], self.o1[:], AF.Silu))

                @block.vector
                def _(v):
                    s_sc.wait(v)
                    s_v.inc(v.tensor_mul(self.o3[:], self.gt[:], self.sq[:]))

            for rnd in range(2):
                with nc.Block() as block:
                    @block.tensor
                    def _(te):
                        s_v.wait(te)
                        for halfi in range(8):
                            i = 8 * rnd + halfi
                            ins = te.transpose(self.pst[:, 128 * halfi:128 * (halfi + 1)],
                                               self.o3[:, 128 * i:128 * (i + 1)], self.ident[:])
                        s_pe.inc(ins)

                    @block.vector
                    def _(v):
                        s_pe.wait(v)
                        for halfi in range(8):
                            i = 8 * rnd + halfi
                            ins = v.tensor_copy(
                                self.ofm[:, 2048 * i + 128 * tt:2048 * i + 128 * (tt + 1)],
                                self.pst[:, 128 * halfi:128 * (halfi + 1)])
                        s_v.inc(ins)

    def wo_residual(self, s):
        p, nc = self.p, self.p.nc
        t = p.t
        s_ld, s_v, s_pe, s_st = self.s_ld, self.s_v, self.s_pe, self.s_st
        for dc in range(8):
            with nc.Block() as block:
                @block.gpsimd
                def _(g):
                    sap = t["WOT"][:, 256 * dc:256 * (dc + 1)].rearrange("(n p) m -> p n m", p=128)
                    dap = self.wbuf[:, 0:ND * 256].rearrange("p (n m) -> p n m", n=ND)
                    self.s_gld.inc(g.dma_start(dap, sap), 16)

            for tt in range(NT):
                with nc.Block() as block:
                    @block.tensor
                    def _(te):
                        s_ld.wait(te); self.s_gld.wait(te)
                        for i in range(ND):
                            ins = te.matmul(
                                self.ps["PSA"][:, 0:256],
                                self.ofm[:, 2048 * i + 128 * tt:2048 * i + 128 * (tt + 1)],
                                self.wbuf[:, 256 * i:256 * (i + 1)],
                                start=(i == 0), stop=(i == ND - 1))
                        s_pe.inc(ins)

                    @block.sync
                    def _(sy):
                        cs = slice(256 * dc, 256 * (dc + 1))
                        s_ld.inc(sy.dma_start(self.xsl[:, 0:256],
                                              t["X"][s, 128 * tt:128 * (tt + 1), cs]), 16)
                        s_ld.inc(sy.dma_start(self.xsl2[:, 0:256],
                                              t["XN_TOK"][s, 128 * tt:128 * (tt + 1), cs]), 16)

                    @block.vector
                    def _(v):
                        s_pe.wait(v)
                        s_ld.wait(v); self.s_gld.wait(v)
                        v.tensor_add(self.xsl[:, 0:256], self.xsl[:, 0:256], self.xsl2[:, 0:256])
                        s_v.inc(v.tensor_add(self.xsl[:, 0:256], self.xsl[:, 0:256],
                                             self.ps["PSA"][:, 0:256]))

                    @block.sync
                    def _(sy):
                        s_v.wait(sy)
                        s_st.inc(sy.dma_start(
                            t["X1"][s, 128 * tt:128 * (tt + 1), 256 * dc:256 * (dc + 1)],
                            self.xsl[:, 0:256]), 16)


def emit_stageB2_all(p):
    b2 = StageB2(p)
    b2.prologue()
    for s in range(NSEQ):
        b2.opost(s)
        b2.wo_residual(s)
    with p.nc.Block() as block:
        @block.sync
        def _(sy):
            b2.s_st.wait(sy)
    return b2



WORLD = list(range(8))
BIGC = 1.0e9


def prep_stageC(inputs):
    A = lambda k: np.asarray(inputs[k])
    d = {}
    d["LNG"] = A("L0_ffn_ln_gamma").reshape(1, D).astype(np.float32)
    d["LNB"] = A("L0_ffn_ln_beta").reshape(1, D).astype(np.float32)
    d["BASES"] = A("L0_ffn_spline_bases").reshape(64, 512).astype(np.float32)
    d["SLOPES"] = A("L0_ffn_spline_slopes").reshape(64, 512).astype(np.float32)
    d["OSCAL"] = A("L0_ffn_output_scale").reshape(64, 1).astype(np.float32)
    d["IOTAR64"] = np.broadcast_to(np.arange(64, dtype=np.float32), (128, 64)).copy()
    d["IOTAE64"] = np.broadcast_to(63.0 - np.arange(64, dtype=np.float32), (128, 64)).copy()
    d["IOTA8E"] = np.broadcast_to(7.0 - np.arange(8, dtype=np.float32), (128, 8)).copy()
    d["IOTAR8"] = np.broadcast_to(np.arange(8, dtype=np.float32), (128, 8)).copy()
    d["CLAR"] = np.broadcast_to((np.arange(64) // 8).astype(np.float32), (128, 64)).copy()
    d["CL8"] = np.zeros((64, 8), dtype=np.float32)
    for t_ in range(64):
        d["CL8"][t_, t_ // 8] = 1.0
    d["IOTA64C"] = np.arange(64, dtype=np.float32).reshape(64, 1)
    return d


STAGEC_IN = ["XN_TOK", "X1", "LNG", "LNB", "BASES", "SLOPES", "OSCAL",
             "IOTAR64", "IOTAE64", "IOTA8E", "IOTAR8", "CLAR", "CL8", "IOTA64C",
             "ONES1", "IDENT", "IDENTF"]


def declare_stageC(p, out):
    def din(name, shape, dtype=F32):
        if name not in p.t:
            p.din(name, shape, dtype)
    din("XN_TOK", (NSEQ, T, D), F32)
    din("LNG", (1, D)); din("LNB", (1, D))
    din("BASES", (64, 512)); din("SLOPES", (64, 512)); din("OSCAL", (64, 1))
    din("IOTAR64", (128, 64)); din("IOTAE64", (128, 64)); din("IOTA8E", (128, 8))
    din("IOTAR8", (128, 8)); din("CLAR", (128, 64)); din("CL8", (64, 8))
    din("IOTA64C", (64, 1))
    din("ONES1", (1, 128)); din("IDENT", (128, 128), BF16); din("IDENTF", (128, 128), F32)
    if "X1" not in p.t:
        p.din("X1", (NSEQ, T, D), F32)
    out("X1C", (NSEQ, T, D), F32)
    p.dscratch("SS_D", (64, 32), F32)
    p.dscratch("CC_IN", (1, 128), F32)
    p.dscratch("CC_OUT", (1, 128), F32)


class StageC:
    def __init__(self, p):
        self.p = p
        scr_pool = [p.sb("scr%d" % i, (128, 2052), F32) for i in range(6)]
        self.gam = scr_pool[3][:, 0:D]
        self.bet = scr_pool[4][:, 0:D]
        self.bases = p.sb("c_bases", (64, 512), F32)
        self.slopes = p.sb("c_slopes", (64, 512), F32)
        self.osc = p.sb("c_osc", (64, 4), F32)
        self.tsig = p.sb("c_tsig", (128, 1024), BF16)
        self.ssb = p.sb("c_ss", (64, 32), F32)
        self.cl8 = p.sb("c_cl8", (64, 8), F32)
        self.iota64c = p.sb("c_i64c", (64, 1), F32)
        self.ones1 = p.sb("b2_ones1", (1, 128), F32)
        self.ident = p.sb("a_id", (128, 128), BF16)
        self.identf = p.sb("a_idf", (128, 128), F32)
        self.ir64 = p.sb("c_ir64", (128, 64), F32)
        self.ie64 = p.sb("c_ie64", (128, 64), F32)
        self.i8e = p.sb("c_i8e", (128, 8), F32)
        self.ir8 = p.sb("c_ir8", (128, 8), F32)
        self.clar = p.sb("c_clar", (128, 64), F32)
        self.xt = p.sb("scr0", (128, 2052), F32)
        self.scr = p.sb("scr1", (128, 2052), F32)
        self.xn2 = p.sb("scr2", (128, 2052), F32)
        self.sgb = p.sb("c_sgb", (128, D), BF16)
        self.sgf = p.sb("c_sgf", (128, D), BF16)
        self.sc_sb = p.sb("c_sc", (64, 128), F32)
        self.cl_sb = p.sb("c_clsb", (8, 128), F32)
        self.sct = p.sb("c_sct", (128, 64), F32)
        self.clt = p.sb("c_clt", (128, 8), F32)
        self.enc8 = p.sb("c_enc8", (128, 8), F32)
        self.m8 = p.sb("c_m8", (128, 1), F32)
        self.bcl = p.sb("c_bcl", (128, 1), F32)
        self.cm = p.sb("c_cm", (128, 64), F32)
        self.e1 = p.sb("c_e1", (128, 64), F32)
        self.bm = p.sb("c_bm", (128, 1), F32)
        self.eqm = p.sb("c_eqm", (128, 64), F32)
        self.tidc = p.sb("c_tid", (128, 32), F32)
        self.tokmin = p.sb("c_tokmin", (128, 32), F32)
        self.tokmax = p.sb("c_tokmax", (128, 32), F32)
        self.xslall = p.sb("c_xslall", (128, 1024), F32)
        self.ohall = p.sb("c_ohall", (128, 2048), F32)
        self.ohtall = p.sb("c_ohtall", (64, 4096), F32)
        self.ln_cols = [p.sb("c_lc%d" % i, (128, 1), F32) for i in range(5)]
        self.trow = p.sb("rowbuf", (1, 512), F32)
        self.mrow = p.sb("c_mrow", (1, 512), F32)
        self.candm = scr_pool[5][0:64, 0:512]
        self.cand2 = scr_pool[5][0:64, 512:1024]
        self.gmin = p.sb("c_gmin", (64, 1), F32)
        self.gmax = p.sb("c_gmax", (64, 1), F32)
        self.acc1 = p.sb("c_acc1", (64, 1), F32)
        self.bg = p.sb("c_bg", (128, 512), F32)
        self.sg = p.sb("c_sg", (128, 512), F32)
        self.mm3 = p.sb("c_mm3", (128, 4), F32)
        self.rec = p.sb("c_rec", (128, 1), F32)
        self.u16 = p.sb("c_u16", (128, 32), F32)
        self.gii = p.sb("c_gii", (128, 32), mybir.dt.int32)
        self.gif = p.sb("c_gif", (128, 32), F32)
        self.eqg = p.sb("c_eqg", (128, 32), F32)
        self.bacc = p.sb("c_bacc", (128, 32), F32)
        self.sacc = p.sb("c_sacc", (128, 32), F32)
        self.tmp32 = p.sb("c_tmp32", (128, 32), F32)
        self.ps = {n: p.ps(n, (128, 512), F32) for n in ("PSA", "PSB", "PSC", "PSD")}
        self.pst = p.ps("PST", (128, 1024), BF16)
        self.s_ld = p.sem("c_ld")
        self.s_gld = p.sem("c_gld")
        self.s_v = p.sem("c_v")
        self.s_sc = p.sem("c_sc")
        self.s_pe = p.sem("c_pe")
        self.s_st = p.sem("c_st")
        self.s_cc = p.sem("c_cc")

    def prologue(self):
        p, nc = self.p, self.p.nc
        t = p.t
        s_ld, s_v, s_sc, s_pe, s_st = (self.s_ld, self.s_v, self.s_sc,
                                       self.s_pe, self.s_st)
        with nc.Block() as block:
            @block.vector
            def _(v):
                self.s_v.inc(v.memset(self.osc[:], 0.0))

            @block.sync
            def _(sy):
                for dst, srcn in ((self.bases, "BASES"), (self.slopes, "SLOPES"),
                                  (self.cl8, "CL8"), (self.iota64c, "IOTA64C"),
                                  (self.ir64, "IOTAR64"), (self.ie64, "IOTAE64"),
                                  (self.i8e, "IOTA8E"), (self.ir8, "IOTAR8"),
                                  (self.clar, "CLAR"), (self.ones1, "ONES1"),
                                  (self.ident, "IDENT"), (self.identf, "IDENTF")):
                    s_ld.inc(sy.dma_start(dst[:], t[srcn][:]), 16)
                self.s_v.wait(sy)
                s_ld.inc(sy.dma_start(self.osc[:, 2:3], t["OSCAL"][:]), 16)

        for dst, srcn in ((self.gam, "LNG"), (self.bet, "LNB")):
            for ccg in range(4):
                with nc.Block() as block:
                    @block.sync
                    def _(sy):
                        s_ld.inc(sy.dma_start(self.trow[:],
                                              t[srcn][0:1, 512 * ccg:512 * (ccg + 1)]), 16)

                    @block.tensor
                    def _(te):
                        s_ld.wait(te)
                        s_pe.inc(te.matmul(self.ps["PSA"][:], self.ones1[:],
                                           self.trow[:], start=True, stop=True))

                    @block.vector
                    def _(v):
                        s_pe.wait(v)
                        s_v.inc(v.tensor_copy(dst[:, 512 * ccg:512 * (ccg + 1)],
                                              self.ps["PSA"][:]))

        with nc.Block() as block:
            @block.vector
            def _(v):
                v.memset(self.tsig[:], 0.0)
                v.reduce_sum(self.ssb[:], self.slopes[:].rearrange(
                    "t (j g) -> t j g", g=16), axis=AX.X)
                s_v.inc(v.engine_nop())

            @block.scalar
            def _(sc):
                s_v.wait(sc)
                s_sc.inc(sc.activation(self.ssb[:], self.ssb[:], AF.Sign))

            @block.sync
            def _(sy):
                s_sc.wait(sy)
                s_st.inc(sy.dma_start(t["SS_D"][:], self.ssb[:]), 16)

        with nc.Block() as block:
            @block.gpsimd
            def _(g):
                self.s_st.wait(g)
                for m in range(4):
                    dap = bass.AP(self.tsig, 32 * m * 1024 + m,
                                  [[1024, 32], [68, 16]])
                    sap = t["SS_D"][m::4, :].rearrange("i j -> j i")
                    self.s_gld.inc(g.dma_start(dap, sap, allow_slow_non_contiguous=True), 16)

    def tile_pass1(self, s, tt):
        p, nc = self.p, self.p.nc
        t = p.t
        s_ld, s_v, s_sc, s_pe, s_st = (self.s_ld, self.s_v, self.s_sc,
                                       self.s_pe, self.s_st)
        gcol = s * NT + tt
        ssum, ssq, rstd, nmean, vtmp = self.ln_cols
        with nc.Block() as block:
            @block.sync
            def _(sy):
                s_ld.inc(sy.dma_start(self.xt[:, 0:D],
                                      t["XN_TOK"][s, 128 * tt:128 * (tt + 1), :]), 16)

            @block.vector
            def _(v):
                s_v.inc(v.memset(ssq[:], 0.0))

            @block.scalar
            def _(sc):
                s_ld.wait(sc); self.s_gld.wait(sc)
                s_v.wait(sc)
                s_sc.inc(sc.activation(self.scr[:, 0:D], self.xt[:, 0:D], AF.Square,
                                       accum_out=ssq[:]))

            @block.vector
            def _(v):
                s_ld.wait(v); self.s_gld.wait(v)
                s_v.inc(v.reduce_sum(ssum[:], self.xt[:, 0:D], axis=AX.X))
                s_v.wait(v)
                s_v.inc(v.tensor_scalar_mul(ssum[:], ssum[:], 1.0 / D))
                s_v.wait(v)
                s_sc.wait(v)
                s_v.inc(v.tensor_scalar_mul(ssq[:], ssq[:], 1.0 / D))
                s_v.inc(v.tensor_mul(vtmp[:], ssum[:], ssum[:]))
                s_v.wait(v)
                s_v.inc(v.tensor_sub(vtmp[:], ssq[:], vtmp[:]))

            @block.scalar
            def _(sc):
                s_v.wait(sc)
                s_sc.inc(sc.activation(vtmp[:], vtmp[:], AF.Ln, bias=EPS_LN))
                s_sc.wait(sc)
                s_sc.inc(sc.activation(rstd[:], vtmp[:], AF.Exp, scale=-0.5))

            @block.vector
            def _(v):
                s_sc.wait(v)
                s_v.inc(v.tensor_mul(nmean[:], ssum[:], rstd[:]))
                s_v.wait(v)
                v.tensor_scalar(self.xn2[:, 0:D], self.xt[:, 0:D], rstd[:], nmean[:],
                                ALU.mult, ALU.subtract)
                v.tensor_mul(self.xn2[:, 0:D], self.xn2[:, 0:D], self.gam[:])
                s_v.inc(v.tensor_add(self.xn2[:, 0:D], self.xn2[:, 0:D], self.bet[:]))

            @block.scalar
            def _(sc):
                s_v.wait(sc)
                s_sc.inc(sc.activation(self.sgb[:], self.xn2[:, 0:D], AF.Sign))

        for rnd in range(2):
            with nc.Block() as block:
                @block.tensor
                def _(te):
                    s_sc.wait(te)
                    for halfi in range(8):
                        i = 8 * rnd + halfi
                        ins = te.transpose(self.pst[:, 128 * halfi:128 * (halfi + 1)],
                                           self.sgb[:, 128 * i:128 * (i + 1)], self.ident[:])
                    s_pe.inc(ins)

                @block.vector
                def _(v):
                    s_pe.wait(v)
                    for halfi in range(8):
                        i = 8 * rnd + halfi
                        ins = v.tensor_copy(self.sgf[:, 128 * i:128 * (i + 1)],
                                            self.pst[:, 128 * halfi:128 * (halfi + 1)])
                    s_v.inc(ins)

        with nc.Block() as block:
            @block.tensor
            def _(te):
                s_v.wait(te)
                for i in range(ND):
                    ins = te.matmul(self.ps["PSA"][0:64, 0:128],
                                    self.tsig[:, 64 * i:64 * (i + 1)],
                                    self.sgf[:, 128 * i:128 * (i + 1)],
                                    start=(i == 0), stop=(i == ND - 1))
                s_pe.inc(ins)

            @block.vector
            def _(v):
                s_pe.wait(v)
                s_v.inc(v.tensor_copy(self.sc_sb[:], self.ps["PSA"][0:64, 0:128]))

        with nc.Block() as block:
            @block.tensor
            def _(te):
                s_v.wait(te)
                s_pe.inc(te.matmul(self.ps["PSB"][0:8, 0:128], self.cl8[:], self.sc_sb[:],
                                   start=True, stop=True))

            @block.vector
            def _(v):
                s_pe.wait(v)
                s_v.inc(v.tensor_copy(self.cl_sb[:], self.ps["PSB"][0:8, 0:128]))

        with nc.Block() as block:
            @block.tensor
            def _(te):
                s_v.wait(te)
                te.transpose(self.ps["PSA"][:, 0:64], self.sc_sb[:], self.identf[0:64, 0:64])
                s_pe.inc(te.transpose(self.ps["PSA"][:, 64:72], self.cl_sb[:],
                                      self.identf[0:8, 0:8]))

            @block.vector
            def _(v):
                s_pe.wait(v)
                v.tensor_copy(self.sct[:], self.ps["PSA"][:, 0:64])
                s_v.inc(v.tensor_copy(self.clt[:], self.ps["PSA"][:, 64:72]))

        with nc.Block() as block:
            @block.vector
            def _(v):
                v.tensor_scalar_mul(self.enc8[:], self.clt[:], 8.0)
                v.tensor_add(self.enc8[:], self.enc8[:], self.i8e[:])
                s_v.inc(v.engine_nop())
                s_v.wait(v)
                s_v.inc(v.reduce_max(self.m8[:], self.enc8[:], axis=AX.X))
                s_v.wait(v)
                s_v.inc(v.tensor_scalar(self.enc8[:], self.enc8[:], self.m8[:], None,
                                        ALU.is_equal))
                s_v.wait(v)
                v.tensor_mul(self.enc8[:], self.enc8[:], self.ir8[:])
                s_v.inc(v.reduce_sum(self.bcl[:], self.enc8[:], axis=AX.X))
                s_v.wait(v)
                s_v.inc(v.tensor_scalar(self.cm[:], self.clar[:], self.bcl[:], None,
                                        ALU.is_equal))
                v.tensor_scalar_mul(self.e1[:], self.sct[:], 64.0)
                v.tensor_add(self.e1[:], self.e1[:], self.ie64[:])
                s_v.wait(v)
                v.scalar_tensor_tensor(self.e1[:], self.cm[:], 65536.0, self.e1[:],
                                       ALU.mult, ALU.add)
                s_v.inc(v.engine_nop())
                s_v.wait(v)
                s_v.inc(v.reduce_max(self.bm[:], self.e1[:], axis=AX.X))
                s_v.wait(v)
                s_v.inc(v.tensor_scalar(self.eqm[:], self.e1[:], self.bm[:], None,
                                        ALU.is_equal))
                s_v.wait(v)
                v.tensor_copy(self.ohall[:, 64 * gcol:64 * (gcol + 1)], self.eqm[:])
                v.tensor_mul(self.e1[:], self.eqm[:], self.ir64[:])
                s_v.inc(v.reduce_sum(self.tidc[:, gcol:gcol + 1], self.e1[:], axis=AX.X))

        with nc.Block() as block:
            @block.vector
            def _(v):
                xsl = self.xslall[:, 32 * gcol:32 * (gcol + 1)]
                v.tensor_scalar_mul(xsl, self.xn2[:, 0:32],
                                    self.ohall[:, 64 * gcol:64 * gcol + 1])
                for t_ in range(1, 64):
                    v.scalar_tensor_tensor(xsl, self.xn2[:, 32 * t_:32 * (t_ + 1)],
                                           self.ohall[:, 64 * gcol + t_:64 * gcol + t_ + 1],
                                           xsl, ALU.mult, ALU.add)
                s_v.inc(v.engine_nop())
                s_v.wait(v)
                v.reduce_sum(self.tokmin[:, gcol:gcol + 1], xsl, axis=AX.X) if False else None
                v.tensor_reduce(self.tokmin[:, gcol:gcol + 1], xsl, axis=AX.X, op=ALU.min)
                s_v.inc(v.tensor_reduce(self.tokmax[:, gcol:gcol + 1], xsl, axis=AX.X,
                                        op=ALU.max))

        with nc.Block() as block:
            @block.tensor
            def _(te):
                s_v.wait(te)
                s_pe.inc(te.transpose(self.ps["PSB"][0:64, 0:128],
                                      self.ohall[:, 64 * gcol:64 * (gcol + 1)],
                                      self.identf[:]))

            @block.vector
            def _(v):
                s_pe.wait(v)
                s_v.inc(v.tensor_copy(self.ohtall[:, 128 * gcol:128 * (gcol + 1)],
                                      self.ps["PSB"][0:64, 0:128]))

    def global_minmax(self):
        p, nc = self.p, self.p.nc
        t = p.t
        s_ld, s_v, s_pe, s_st, s_cc = (self.s_ld, self.s_v, self.s_pe,
                                       self.s_st, self.s_cc)
        for which in ("min", "max"):
            valsrc = self.tokmin if which == "min" else self.tokmax
            out = self.gmin if which == "min" else self.gmax
            op = ALU.min if which == "min" else ALU.max
            fill = BIGC if which == "min" else -BIGC
            for g in range(8):
                with nc.Block() as block:
                    @block.tensor
                    def _(te):
                        s_v.wait(te)
                        for j in range(4):
                            te.matmul(self.ps["PSC"][0:1, 128 * j:128 * (j + 1)],
                                      self.tidc[:, 4 * g + j:4 * g + j + 1],
                                      self.identf[:], start=True, stop=True)
                            ins = te.matmul(self.ps["PSD"][0:1, 128 * j:128 * (j + 1)],
                                            valsrc[:, 4 * g + j:4 * g + j + 1],
                                            self.identf[:], start=True, stop=True)
                        s_pe.inc(ins)

                    @block.vector
                    def _(v):
                        s_pe.wait(v)
                        v.tensor_copy(self.trow[:], self.ps["PSC"][0:1, 0:512])
                        s_v.inc(v.tensor_copy(self.mrow[:], self.ps["PSD"][0:1, 0:512]))

                with nc.Block() as block:
                    @block.tensor
                    def _(te):
                        s_v.wait(te)
                        te.matmul(self.ps["PSA"][0:64, :], self.ones1[0:1, 0:64],
                                  self.trow[:], start=True, stop=True)
                        s_pe.inc(te.matmul(self.ps["PSB"][0:64, :], self.ones1[0:1, 0:64],
                                           self.mrow[:], start=True, stop=True))

                    @block.vector
                    def _(v):
                        s_pe.wait(v)
                        s_v.inc(v.tensor_scalar(self.candm[:], self.ps["PSA"][0:64, :],
                                                self.iota64c[:], None, ALU.is_equal))
                        s_v.wait(v)
                        v.tensor_tensor(self.cand2[:], self.candm[:], self.ps["PSB"][0:64, :],
                                        ALU.mult)
                        v.tensor_scalar(self.candm[:], self.candm[:], -fill, fill,
                                        ALU.mult, ALU.add)
                        s_v.inc(v.tensor_add(self.cand2[:], self.cand2[:], self.candm[:]))
                        s_v.wait(v)
                        s_v.inc(v.tensor_reduce(self.acc1[:], self.cand2[:], axis=AX.X,
                                                op=op))
                        s_v.wait(v)
                        if g == 0:
                            s_v.inc(v.tensor_copy(out[:], self.acc1[:]))
                        else:
                            s_v.inc(v.tensor_tensor(out[:], out[:], self.acc1[:], op))
                        s_v.wait(v)

        with nc.Block() as block:
            @block.vector
            def _(v):
                s_v.inc(v.tensor_scalar_mul(self.gmin[:], self.gmin[:], -1.0))

            @block.sync
            def _(sy):
                s_v.wait(sy)
                s_st.inc(sy.dma_start(t["CC_IN"][0:1, 0:64], self.gmin[:]), 16)
                s_st.inc(sy.dma_start(t["CC_IN"][0:1, 64:128], self.gmax[:]), 16)

        with nc.Block() as block:
            @block.gpsimd
            def _(g):
                s_st.wait(g)
                g.collective_compute("AllReduce", ALU.max, replica_groups=[WORLD],
                                     ins=[t["CC_IN"].ap().opt()],
                                     outs=[t["CC_OUT"].ap().opt()]).then_inc(s_cc.h)
                s_cc.n += 1
                s_cc.wait(g)

        with nc.Block() as block:
            @block.sync
            def _(sy):
                s_ld.inc(sy.dma_start(self.gmin[:], t["CC_OUT"][0:1, 0:64]), 16)
                s_ld.inc(sy.dma_start(self.gmax[:], t["CC_OUT"][0:1, 64:128]), 16)

            @block.vector
            def _(v):
                s_ld.wait(v); self.s_gld.wait(v)
                v.tensor_scalar_mul(self.osc[:, 0:1], self.gmin[:], -1.0)
                s_v.inc(v.tensor_copy(self.osc[:, 1:2], self.gmax[:]))

    def tile_pass2(self, s, tt):
        p, nc = self.p, self.p.nc
        t = p.t
        s_ld, s_v, s_sc, s_pe, s_st = (self.s_ld, self.s_v, self.s_sc,
                                       self.s_pe, self.s_st)
        gcol = s * NT + tt
        with nc.Block() as block:
            @block.tensor
            def _(te):
                oht = self.ohtall[:, 128 * gcol:128 * (gcol + 1)]
                te.matmul(self.ps["PSA"][:], oht, self.bases[:], start=True, stop=True)
                s_pe.inc(te.matmul(self.ps["PSB"][:, 0:4], oht, self.osc[:],
                                   start=True, stop=True))

            @block.sync
            def _(sy):
                s_ld.inc(sy.dma_start(self.xt[:, 0:D],
                                      t["X1"][s, 128 * tt:128 * (tt + 1), :]), 16)

            @block.vector
            def _(v):
                s_pe.wait(v)
                v.tensor_copy(self.bg[:], self.ps["PSA"][:])
                s_v.inc(v.tensor_copy(self.mm3[:], self.ps["PSB"][:, 0:4]))

        with nc.Block() as block:
            @block.tensor
            def _(te):
                s_v.wait(te)
                oht = self.ohtall[:, 128 * gcol:128 * (gcol + 1)]
                s_pe.inc(te.matmul(self.ps["PSA"][:], oht, self.slopes[:],
                                   start=True, stop=True))

            @block.vector
            def _(v):
                s_pe.wait(v)
                s_v.inc(v.tensor_copy(self.sg[:], self.ps["PSA"][:]))
                s_v.inc(v.tensor_sub(self.rec[:], self.mm3[:, 1:2], self.mm3[:, 0:1]))
                s_v.wait(v)
                s_v.inc(v.tensor_scalar_add(self.rec[:], self.rec[:], 1e-8))
                s_v.wait(v)
                s_v.inc(v.reciprocal(self.rec[:], self.rec[:]))
                s_v.wait(v)
                s_v.inc(v.tensor_scalar_mul(self.rec[:], self.rec[:], 16.0))
                s_v.wait(v)
                xsl = self.xslall[:, 32 * gcol:32 * (gcol + 1)]
                v.tensor_scalar(self.u16[:], xsl, self.mm3[:, 0:1], self.rec[:],
                                ALU.subtract, ALU.mult)
                s_v.inc(v.engine_nop())
                s_v.wait(v)
                s_v.inc(v.tensor_scalar(self.u16[:], self.u16[:], 0.0, 15.999984741210938,
                                        ALU.max, ALU.min))
                s_v.wait(v)
                v.tensor_copy(self.gii[:], self.u16[:])
                s_v.inc(v.engine_nop())
                s_v.wait(v)
                s_v.inc(v.tensor_copy(self.gif[:], self.gii[:]))
                s_v.wait(v)
                v.memset(self.bacc[:], 0.0)
                s_v.inc(v.memset(self.sacc[:], 0.0))
                for gg in range(16):
                    s_v.wait(v)
                    s_v.inc(v.tensor_scalar(self.eqg[:], self.gif[:], float(gg), None,
                                            ALU.is_equal))
                    s_v.wait(v)
                    v.tensor_tensor(self.tmp32[:], self.eqg[:],
                                    self.bg[:, gg::16], ALU.mult)
                    v.tensor_add(self.bacc[:], self.bacc[:], self.tmp32[:])
                    v.tensor_tensor(self.tmp32[:], self.eqg[:],
                                    self.sg[:, gg::16], ALU.mult)
                    s_v.inc(v.tensor_add(self.sacc[:], self.sacc[:], self.tmp32[:]))
                s_v.wait(v)
                v.tensor_sub(self.tmp32[:], self.u16[:], self.gif[:])
                s_v.inc(v.engine_nop())
                s_v.wait(v)
                v.tensor_mul(self.tmp32[:], self.tmp32[:], self.sacc[:])
                s_v.inc(v.engine_nop())
                s_v.wait(v)
                v.tensor_add(self.tmp32[:], self.tmp32[:], self.bacc[:])
                s_v.inc(v.engine_nop())
                s_v.wait(v)
                s_v.inc(v.tensor_scalar_mul(self.tmp32[:], self.tmp32[:], self.mm3[:, 2:3]))
                s_v.wait(v)
                s_ld.wait(v); self.s_gld.wait(v)
                for t_ in range(64):
                    v.scalar_tensor_tensor(self.xt[:, 32 * t_:32 * (t_ + 1)],
                                           self.tmp32[:],
                                           self.ohall[:, 64 * gcol + t_:64 * gcol + t_ + 1],
                                           self.xt[:, 32 * t_:32 * (t_ + 1)],
                                           ALU.mult, ALU.add)
                s_v.inc(v.engine_nop())

            @block.sync
            def _(sy):
                s_v.wait(sy)
                s_st.inc(sy.dma_start(t["X1C"][s, 128 * tt:128 * (tt + 1), :],
                                      self.xt[:, 0:D]), 16)


def emit_stageC_all(p):
    c = StageC(p)
    c.prologue()
    for s in range(NSEQ):
        for tt in range(NT):
            c.tile_pass1(s, tt)
    c.global_minmax()
    for s in range(NSEQ):
        for tt in range(NT):
            c.tile_pass2(s, tt)
    with p.nc.Block() as block:
        @block.sync
        def _(sy):
            c.s_st.wait(sy)
    return c



SQ128 = float(np.sqrt(np.float32(128.0)))


def prep_stageD(inputs):
    A = lambda k: np.asarray(inputs[k])
    return {"WQ1": A("L1_attn_Wq").astype(np.float32),
            "WK1": A("L1_attn_Wk").astype(np.float32),
            "WV1": A("L1_attn_Wv").astype(np.float32),
            "WO1": A("L1_attn_Wo").astype(np.float32),
            "WR1": A("L1_ffn_router_w").astype(np.float32)}


def declare_stageD(p, out):
    def din(name, shape, dtype=F32):
        if name not in p.t:
            p.din(name, shape, dtype)
    din("X1C", (NSEQ, T, D), F32) if "X1C" not in p.t else None
    din("WQ1", (D, D)); din("WK1", (D, 512)); din("WV1", (D, 512))
    din("WR1", (D, 8))
    din("IDENT", (128, 128), BF16)
    din("IDENTF", (128, 128), F32)
    din("ONES1", (1, 128), F32)
    out("XN1FM", (NSEQ, D, T), BF16)
    out("XSFM", (NSEQ, D, T), BF16)
    out("QF1", (NSEQ, H1, DH1, T), BF16)
    out("KF1", (NSEQ, KVH, DH1, T), BF16)
    out("VT1", (NSEQ, T, 512), BF16)


class StageD:
    def __init__(self, p):
        self.p = p
        self.xn_fm = p.sb("a_xnfm", (128, ND * 2048), BF16)
        self.wbuf = p.sb("a_wbuf", (128, 4096), BF16)
        self.ident = p.sb("a_id", (128, 128), BF16)
        self.identf = p.sb("a_idf", (128, 128), F32)
        self.ones1 = p.sb("b2_ones1", (1, 128), F32)
        self.xt = p.sb("scr0", (128, 2052), F32)
        self.scr = p.sb("scr1", (128, 2052), F32)
        self.xnt = p.sb("scr2", (128, 2052), F32)
        self.xnb = p.sb("ln_xnb", (128, D), BF16)
        self.lncols = [p.sb("d_lc%d" % i, (128, 1), F32) for i in range(5)]
        self.logits = p.sb("d_logits", (8, NSEQ * T), F32)
        self.lgt = p.sb("d_lgt", (128, 8 * 32), F32)
        self.maxl = p.sb("d_maxl", (128, 32), F32)
        self.score = p.sb("d_score", (128, 32), F32)
        self.alr = p.sb("d_alr", (128, 32), F32)
        self.maskc = p.sb("d_maskc", (128, 32 * 8), F32)
        self.eqs = p.sb("d_eqs", (128, 32), F32)
        self.srow = p.sb("c_trow", (1, 4096), F32)
        self.stage = p.sb("d_stage", (128, 512), BF16)
        self.ps = {n: p.ps(n, (128, 512), F32) for n in ("PSA", "PSB")}
        self.pst = p.ps("PST", (128, 1024), BF16)
        self.s_ld = p.sem("d_ld")
        self.s_gld = p.sem("d_gld")
        self.s_v = p.sem("d_v")
        self.s_sc = p.sem("d_sc")
        self.s_pe = p.sem("d_pe")
        self.s_st = p.sem("d_st")

    def ln_seq(self, s):
        p, nc = self.p, self.p.nc
        t = p.t
        s_ld, s_v, s_sc, s_pe, s_st = (self.s_ld, self.s_v, self.s_sc,
                                       self.s_pe, self.s_st)
        ssum, ssq, rstd, nmean, vtmp = self.lncols
        for tt in range(NT):
            with nc.Block() as block:
                @block.sync
                def _(sy):
                    s_ld.inc(sy.dma_start(self.xt[:, 0:D],
                                          t["X1C"][s, 128 * tt:128 * (tt + 1), :]), 16)

                @block.vector
                def _(v):
                    s_v.inc(v.memset(ssq[:], 0.0))

                @block.scalar
                def _(sc):
                    s_ld.wait(sc)
                    s_v.wait(sc)
                    s_sc.inc(sc.activation(self.scr[:, 0:D], self.xt[:, 0:D], AF.Square,
                                           accum_out=ssq[:]))

                @block.vector
                def _(v):
                    s_ld.wait(v)
                    s_v.inc(v.reduce_sum(ssum[:], self.xt[:, 0:D], axis=AX.X))
                    s_v.wait(v)
                    s_v.inc(v.tensor_scalar_mul(ssum[:], ssum[:], 1.0 / D))
                    s_v.wait(v)
                    s_sc.wait(v)
                    s_v.inc(v.tensor_scalar_mul(ssq[:], ssq[:], 1.0 / D))
                    s_v.inc(v.tensor_mul(vtmp[:], ssum[:], ssum[:]))
                    s_v.wait(v)
                    s_v.inc(v.tensor_sub(vtmp[:], ssq[:], vtmp[:]))

                @block.scalar
                def _(sc):
                    s_v.wait(sc)
                    s_sc.inc(sc.activation(vtmp[:], vtmp[:], AF.Ln, bias=EPS_LN))
                    s_sc.wait(sc)
                    s_sc.inc(sc.activation(rstd[:], vtmp[:], AF.Exp, scale=-0.5))

                @block.vector
                def _(v):
                    s_sc.wait(v)
                    s_v.inc(v.tensor_mul(nmean[:], ssum[:], rstd[:]))
                    s_v.wait(v)
                    v.tensor_scalar(self.xnt[:, 0:D], self.xt[:, 0:D], rstd[:], nmean[:],
                                    ALU.mult, ALU.subtract)
                    s_v.inc(v.tensor_copy(self.xnb[:], self.xnt[:, 0:D]))

                @block.tensor
                def _(te):
                    s_v.wait(te)
                    for halfi in range(8):
                        ins = te.transpose(self.pst[:, 128 * halfi:128 * (halfi + 1)],
                                           self.xnb[:, 128 * halfi:128 * (halfi + 1)],
                                           self.ident[:])
                    s_pe.inc(ins)

            with nc.Block() as block:
                @block.vector
                def _(v):
                    s_pe.wait(v)
                    for halfi in range(8):
                        ins = v.tensor_copy(
                            self.xn_fm[:, 2048 * halfi + 128 * tt:2048 * halfi + 128 * (tt + 1)],
                            self.pst[:, 128 * halfi:128 * (halfi + 1)])
                    s_v.inc(ins)

                @block.tensor
                def _(te):
                    s_v.wait(te)
                    for halfi in range(8):
                        i = 8 + halfi
                        ins = te.transpose(self.pst[:, 128 * halfi:128 * (halfi + 1)],
                                           self.xnb[:, 128 * i:128 * (i + 1)], self.ident[:])
                    s_pe.inc(ins)

            with nc.Block() as block:
                @block.vector
                def _(v):
                    s_pe.wait(v)
                    for halfi in range(8):
                        i = 8 + halfi
                        ins = v.tensor_copy(
                            self.xn_fm[:, 2048 * i + 128 * tt:2048 * i + 128 * (tt + 1)],
                            self.pst[:, 128 * halfi:128 * (halfi + 1)])
                    s_v.inc(ins)

        with nc.Block() as block:
            @block.sync
            def _(sy):
                s_v.wait(sy)
                for i in range(ND):
                    s_st.inc(sy.dma_start(t["XN1FM"][s, 128 * i:128 * (i + 1), :],
                                          self.xn_fm[:, 2048 * i:2048 * (i + 1)]), 16)

    def proj_seq(self, s):
        p, nc = self.p, self.p.nc
        t = p.t
        s_ld, s_gld, s_v, s_pe, s_st = (self.s_ld, self.s_gld, self.s_v,
                                        self.s_pe, self.s_st)
        for wname, nout, dst, scale in (("WQ1", 16, "QF1", SQ128),
                                        ("WK1", 4, "KF1", 1.0)):
            for j in range(nout):
                with nc.Block() as block:
                    @block.gpsimd
                    def _(g):
                        sap = t[wname][:, 128 * j:128 * (j + 1)].rearrange(
                            "(n p) m -> p n m", p=128)
                        dap = self.wbuf[:, 0:ND * 128].rearrange("p (n m) -> p n m", n=ND)
                        s_gld.inc(g.dma_start(dap, sap), 16)

                for cc in range(4):
                    with nc.Block() as block:
                        @block.tensor
                        def _(te):
                            s_gld.wait(te)
                            for i in range(ND):
                                ins = te.matmul(
                                    self.ps["PSA"][:],
                                    self.wbuf[:, 128 * i:128 * (i + 1)],
                                    self.xn_fm[:, 2048 * i + 512 * cc:2048 * i + 512 * (cc + 1)],
                                    start=(i == 0), stop=(i == ND - 1))
                            s_pe.inc(ins)

                        @block.vector
                        def _(v):
                            s_pe.wait(v)
                            s_st.wait(v, lag=16)
                            s_v.inc(v.tensor_scalar_mul(self.stage[:], self.ps["PSA"][:],
                                                        scale))

                        @block.sync
                        def _(sy):
                            s_v.wait(sy)
                            s_st.inc(sy.dma_start(
                                t[dst][s, j, :, 512 * cc:512 * (cc + 1)], self.stage[:]), 16)

        for dc in range(2):
            with nc.Block() as block:
                @block.gpsimd
                def _(g):
                    sap = t["WV1"][:, 256 * dc:256 * (dc + 1)].rearrange("(n p) m -> p n m", p=128)
                    dap = self.wbuf[:, 0:ND * 256].rearrange("p (n m) -> p n m", n=ND)
                    s_gld.inc(g.dma_start(dap, sap), 16)

            for tt in range(NT):
                with nc.Block() as block:
                    @block.tensor
                    def _(te):
                        s_gld.wait(te)
                        for i in range(ND):
                            ins = te.matmul(
                                self.ps["PSA"][:, 0:256],
                                self.xn_fm[:, 2048 * i + 128 * tt:2048 * i + 128 * (tt + 1)],
                                self.wbuf[:, 256 * i:256 * (i + 1)],
                                start=(i == 0), stop=(i == ND - 1))
                        s_pe.inc(ins)

                    @block.vector
                    def _(v):
                        s_pe.wait(v)
                        s_st.wait(v, lag=16)
                        s_v.inc(v.tensor_copy(self.stage[:, 0:256], self.ps["PSA"][:, 0:256]))

                    @block.sync
                    def _(sy):
                        s_v.wait(sy)
                        s_st.inc(sy.dma_start(
                            t["VT1"][s, 128 * tt:128 * (tt + 1), 256 * dc:256 * (dc + 1)],
                            self.stage[:, 0:256]), 16)

        with nc.Block() as block:
            @block.gpsimd
            def _(g):
                sap = t["WR1"][:, :].rearrange("(n p) m -> p n m", p=128)
                dap = self.wbuf[:, 0:ND * 8].rearrange("p (n m) -> p n m", n=ND)
                s_gld.inc(g.dma_start(dap, sap), 16)

        for cc in range(4):
            with nc.Block() as block:
                @block.tensor
                def _(te):
                    s_gld.wait(te)
                    for i in range(ND):
                        ins = te.matmul(
                            self.ps["PSA"][0:8, :],
                            self.wbuf[:, 8 * i:8 * (i + 1)],
                            self.xn_fm[:, 2048 * i + 512 * cc:2048 * i + 512 * (cc + 1)],
                            start=(i == 0), stop=(i == ND - 1))
                    s_pe.inc(ins)

                @block.vector
                def _(v):
                    s_pe.wait(v)
                    s_v.inc(v.tensor_copy(
                        self.logits[:, T * s + 512 * cc:T * s + 512 * (cc + 1)],
                        self.ps["PSA"][0:8, :]))

    def router_post(self):
        p, nc = self.p, self.p.nc
        t = p.t
        s_ld, s_gld, s_v, s_sc, s_pe, s_st = (self.s_ld, self.s_gld, self.s_v,
                                              self.s_sc, self.s_pe, self.s_st)
        for g8 in range(8):
            with nc.Block() as block:
                @block.tensor
                def _(te):
                    s_v.wait(te)
                    for j in range(4):
                        gcol = 4 * g8 + j
                        ins = te.transpose(self.ps["PSA"][:, 8 * j:8 * (j + 1)],
                                           self.logits[:, 128 * gcol:128 * (gcol + 1)],
                                           self.identf[0:8, 0:8])
                    s_pe.inc(ins)

                @block.vector
                def _(v):
                    s_pe.wait(v)
                    for j in range(4):
                        gcol = 4 * g8 + j
                        ins = v.tensor_copy(self.lgt[:, 8 * gcol:8 * (gcol + 1)],
                                            self.ps["PSA"][:, 8 * j:8 * (j + 1)])
                    s_v.inc(ins)

        with nc.Block() as block:
            @block.vector
            def _(v):
                s_v.inc(v.tensor_reduce(self.maxl[:], self.lgt[:].rearrange(
                    "p (c e) -> p c e", e=8), axis=AX.X, op=ALU.max))
                s_v.wait(v)
                s_v.inc(v.memset(self.alr[:], 0.0))

            @block.scalar
            def _(sc):
                s_v.wait(sc)
                s_sc.inc(sc.activation(self.score[:], self.maxl[:], AF.Sigmoid))

            @block.vector
            def _(v):
                s_sc.wait(v)
                for e in range(8):
                    v.tensor_tensor(self.eqs[:], self.lgt[:, e::8], self.maxl[:],
                                    ALU.is_equal)
                    s_v.inc(v.engine_nop())
                    s_v.wait(v)
                    v.tensor_scalar(self.maskc[:, 32 * e:32 * (e + 1)], self.alr[:],
                                    0.5, None, ALU.is_lt)
                    s_v.inc(v.engine_nop())
                    s_v.wait(v)
                    v.tensor_mul(self.maskc[:, 32 * e:32 * (e + 1)],
                                 self.maskc[:, 32 * e:32 * (e + 1)], self.eqs[:])
                    s_v.inc(v.engine_nop())
                    s_v.wait(v)
                    s_v.inc(v.tensor_add(self.alr[:], self.alr[:],
                                         self.maskc[:, 32 * e:32 * (e + 1)]))
                    s_v.wait(v)

        for g8 in range(8):
            with nc.Block() as block:
                @block.tensor
                def _(te):
                    for j in range(4):
                        gcol = 4 * g8 + j
                        ins = te.matmul(self.ps["PSA"][0:1, 128 * j:128 * (j + 1)],
                                        self.score[:, gcol:gcol + 1],
                                        self.identf[:],
                                        start=True, stop=True)
                    s_pe.inc(ins)

                @block.vector
                def _(v):
                    s_pe.wait(v)
                    s_v.inc(v.tensor_copy(self.srow[0:1, 512 * g8:512 * (g8 + 1)],
                                          self.ps["PSA"][0:1, 0:512]))

        for s in range(NSEQ):
            for cc in range(4):
                with nc.Block() as block:
                    @block.tensor
                    def _(te):
                        s_v.wait(te)
                        s_pe.inc(te.matmul(
                            self.ps["PSB"][:], self.ones1[:],
                            self.srow[0:1, 2048 * s + 512 * cc:2048 * s + 512 * (cc + 1)],
                            start=True, stop=True))

                for i in range(ND):
                    with nc.Block() as block:
                        @block.sync
                        def _(sy):
                            s_st.wait(sy)
                            s_ld.inc(sy.dma_start(
                                self.stage[:],
                                t["XN1FM"][s, 128 * i:128 * (i + 1), 512 * cc:512 * (cc + 1)]), 16)

                        @block.vector
                        def _(v):
                            s_ld.wait(v)
                            s_pe.wait(v)
                            s_v.inc(v.tensor_tensor(self.stage[:], self.stage[:],
                                                    self.ps["PSB"][:], ALU.mult))

                        @block.sync
                        def _(sy):
                            s_v.wait(sy)
                            s_st.inc(sy.dma_start(
                                t["XSFM"][s, 128 * i:128 * (i + 1), 512 * cc:512 * (cc + 1)],
                                self.stage[:]), 16)


def emit_stageD_all(p):
    d = StageD(p)
    with p.nc.Block() as block:
        @block.sync
        def _(sy):
            d.s_ld.inc(sy.dma_start(d.ident[:], p.t["IDENT"][:]), 16)
            d.s_ld.inc(sy.dma_start(d.identf[:], p.t["IDENTF"][:]), 16)
            d.s_ld.inc(sy.dma_start(d.ones1[:], p.t["ONES1"][:]), 16)
            d.s_ld.wait(sy)
    for s in range(NSEQ):
        d.ln_seq(s)
        d.proj_seq(s)
    d.router_post()
    with p.nc.Block() as block:
        @block.sync
        def _(sy):
            d.s_st.wait(sy)
    return d



def prep_stageE():
    masks = {}
    kt = np.arange(128)[:, None]
    qq = np.arange(512)[None, :]
    for dvar in range(4):
        m = np.where(qq >= kt + 128 * dvar, 0.0, NEG_BIG).astype(np.float32)
        masks["MASKD%d" % dvar] = m
    return masks


NEG_BIG = -1.0e9


def declare_stageE(p, out):
    def din(name, shape, dtype=F32):
        if name not in p.t:
            p.din(name, shape, dtype)
    din("WO1", (D, D))
    for dvar in range(4):
        din("MASKD%d" % dvar, (128, 512))
    din("ONESC", (128, 1), BF16)
    din("ONESR", (1, 128), BF16)
    out("OUT", (NSEQ, T, D), F32)


class StageE:
    def __init__(self, p):
        self.p = p
        self.ctxfm = p.sb("a_xnfm", (128, ND * 2048), BF16)
        self.wbuf = p.sb("a_wbuf", (128, 4096), BF16)
        self.kfm = p.sb("e_kfm", (128, T), BF16)
        self.vtk = p.sb("e_vtk", (128, T), BF16)
        self.qfm = p.sb("e_qfm", (128, T), BF16)
        self.pbf = p.sb("e_pbf", (128, 512), BF16)
        self.clp = p.sb("e_clp", (128, 512), F32)
        self.sufc = p.sb("e_sufc", (128, 8), F32)
        self.numf = p.sb("e_numf", (128, 512), F32)
        self.denr = p.sb("e_denr", (1, 512), F32)
        self.masks = [p.sb("e_maskd%d" % dv, (128, 512), F32) for dv in range(4)]
        self.onesc = p.sb("e_onesc", (128, 1), BF16)
        self.onesr = p.sb("e_onesr", (1, 128), BF16)
        self.ones1 = p.sb("b2_ones1", (1, 128), F32)
        self.xsl = p.sb("b2_xsl", (128, 512), F32)
        self.stage = p.sb("d_stage", (128, 512), BF16)
        self.ps = {n: p.ps(n, (128, 512), F32) for n in
                   ("PSA", "PSB", "PSC", "PSD")}
        self.s_ld = p.sem("e_ld")
        self.s_gld = p.sem("e_gld")
        self.s_v = p.sem("e_v")
        self.s_sc = p.sem("e_sc")
        self.s_pe = p.sem("e_pe")
        self.s_st = p.sem("e_st")

    def load_consts(self):
        p, nc = self.p, self.p.nc
        with nc.Block() as block:
            @block.sync
            def _(sy):
                for dv in range(4):
                    self.s_ld.inc(sy.dma_start(self.masks[dv][:], p.t["MASKD%d" % dv][:]), 16)
                self.s_ld.inc(sy.dma_start(self.onesc[:], p.t["ONESC"][:]), 16)
                self.s_ld.inc(sy.dma_start(self.onesr[:], p.t["ONESR"][:]), 16)

    def head(self, s, h):
        p, nc = self.p, self.p.nc
        t = p.t
        s_ld, s_v, s_sc, s_pe, s_st = (self.s_ld, self.s_v, self.s_sc,
                                       self.s_pe, self.s_st)
        kv = h // 4
        PSA, PSB, PSC, PSD = (self.ps[n] for n in ("PSA", "PSB", "PSC", "PSD"))
        if h % 4 == 0:
            with nc.Block() as block:
                @block.sync
                def _(sy):
                    s_ld.inc(sy.dma_start(self.kfm[:], t["KF1"][s, kv]), 16)
                    sap = t["VT1"][s][:, 128 * kv:128 * (kv + 1)].rearrange(
                        "(tt pp) d2 -> pp tt d2", pp=128)
                    dap = self.vtk[:].rearrange("pp (tt d2) -> pp tt d2", tt=NT)
                    s_ld.inc(sy.dma_start(dap, sap), 16)

            with nc.Block() as block:
                @block.tensor
                def _(te):
                    s_ld.wait(te)
                    for b in range(4):
                        for j in range(4):
                            tti = 4 * b + j
                            ins = te.matmul(PSA[:, b:b + 1],
                                            self.vtk[:, 128 * tti:128 * (tti + 1)],
                                            self.onesc[:],
                                            start=(j == 0), stop=(j == 3))
                    s_pe.inc(ins)

                @block.vector
                def _(v):
                    s_pe.wait(v)
                    s_v.inc(v.tensor_copy(self.sufc[:, 0:4], PSA[:, 0:4]))
                    s_v.wait(v)
                    v.memset(self.sufc[:, 7:8], 0.0)
                    s_v.inc(v.engine_nop())
                    s_v.wait(v)
                    for qc in (2, 1, 0):
                        s_v.inc(v.tensor_add(self.sufc[:, 4 + qc:5 + qc],
                                             self.sufc[:, 5 + qc:6 + qc],
                                             self.sufc[:, qc + 1:qc + 2]))
                        s_v.wait(v)
                    s_v.inc(v.tensor_scalar_mul(self.sufc[:, 4:8], self.sufc[:, 4:8], E_M50))

        with nc.Block() as block:
            @block.sync
            def _(sy):
                s_ld.inc(sy.dma_start(self.qfm[:], t["QF1"][s, h]), 16)

        for qc in range(4):
            with nc.Block() as block:
                @block.tensor
                def _(te):
                    s_ld.wait(te)
                    pass

                @block.vector
                def _(v):
                    pass

            nkb = 4 * qc + 4
            for kb in range(nkb):
                dvar = kb - 4 * qc
                with nc.Block() as block:
                    @block.tensor
                    def _(te):
                        s_ld.wait(te)
                        s_v.wait(te)
                        s_pe.inc(te.matmul(PSA[:], self.kfm[:, 128 * kb:128 * (kb + 1)],
                                           self.qfm[:, 512 * qc:512 * (qc + 1)],
                                           start=True, stop=True))

                    @block.vector
                    def _(v):
                        s_pe.wait(v)
                        if dvar >= 0:
                            v.tensor_add(self.clp[:], PSA[:], self.masks[dvar][:])
                            s_v.inc(v.engine_nop())
                            s_v.wait(v)
                            s_v.inc(v.tensor_scalar(self.clp[:], self.clp[:], -50.0, 50.0,
                                                    ALU.max, ALU.min))
                        else:
                            s_v.inc(v.tensor_scalar(self.clp[:], PSA[:], -50.0, 50.0,
                                                    ALU.max, ALU.min))

                    @block.scalar
                    def _(sc):
                        s_v.wait(sc)
                        s_sc.inc(sc.activation(self.pbf[:], self.clp[:], AF.Exp))

                    @block.tensor
                    def _(te):
                        s_sc.wait(te)
                        te.matmul(PSB[:], self.vtk[:, 128 * kb:128 * (kb + 1)],
                                  self.pbf[:], start=(kb == 0), stop=(kb == nkb - 1))
                        s_pe.inc(te.matmul(PSC[0:1, :], self.onesc[:], self.pbf[:],
                                           start=(kb == 0), stop=(kb == nkb - 1)))

            with nc.Block() as block:
                @block.vector
                def _(v):
                    s_pe.wait(v)
                    cb = E_M50 * float(2048 - 512 * (qc + 1))
                    s_v.inc(v.tensor_scalar_add(self.denr[:], PSC[0:1, :], cb))
                    s_v.wait(v)
                    s_v.inc(v.reciprocal(self.denr[:], self.denr[:]))
                    s_v.wait(v)
                    s_v.inc(v.tensor_scalar(self.numf[:], PSB[:], self.sufc[:, 4 + qc:5 + qc],
                                            None, ALU.add))

                @block.tensor
                def _(te):
                    s_v.wait(te)
                    s_pe.inc(te.matmul(PSD[:], self.ones1[:], self.denr[:],
                                       start=True, stop=True))

                @block.vector
                def _(v):
                    s_pe.wait(v)
                    s_v.inc(v.tensor_tensor(
                        self.ctxfm[:, 2048 * h + 512 * qc:2048 * h + 512 * (qc + 1)],
                        self.numf[:], PSD[:], ALU.mult))

    def wo_residual(self, s):
        p, nc = self.p, self.p.nc
        t = p.t
        s_ld, s_gld, s_v, s_pe, s_st = (self.s_ld, self.s_gld, self.s_v,
                                        self.s_pe, self.s_st)
        for dc in range(8):
            with nc.Block() as block:
                @block.gpsimd
                def _(g):
                    sap = t["WO1"][:, 256 * dc:256 * (dc + 1)].rearrange("(n p) m -> p n m", p=128)
                    dap = self.wbuf[:, 0:ND * 256].rearrange("p (n m) -> p n m", n=ND)
                    s_gld.inc(g.dma_start(dap, sap), 16)

            for tt in range(NT):
                with nc.Block() as block:
                    @block.tensor
                    def _(te):
                        s_gld.wait(te)
                        for i in range(ND):
                            ins = te.matmul(
                                self.ps["PSA"][:, 0:256],
                                self.ctxfm[:, 2048 * i + 128 * tt:2048 * i + 128 * (tt + 1)],
                                self.wbuf[:, 256 * i:256 * (i + 1)],
                                start=(i == 0), stop=(i == ND - 1))
                        s_pe.inc(ins)

                    @block.sync
                    def _(sy):
                        cs = slice(256 * dc, 256 * (dc + 1))
                        s_ld.inc(sy.dma_start(self.xsl[:, 0:256],
                                              t["X1C"][s, 128 * tt:128 * (tt + 1), cs]), 16)

                    @block.vector
                    def _(v):
                        s_pe.wait(v)
                        s_ld.wait(v)
                        s_v.inc(v.tensor_add(self.xsl[:, 0:256], self.xsl[:, 0:256],
                                             self.ps["PSA"][:, 0:256]))

                    @block.sync
                    def _(sy):
                        s_v.wait(sy)
                        s_st.inc(sy.dma_start(
                            t["OUT"][s, 128 * tt:128 * (tt + 1), 256 * dc:256 * (dc + 1)],
                            self.xsl[:, 0:256]), 16)


def emit_stageE_all(p):
    e = StageE(p)
    e.load_consts()
    for s in range(NSEQ):
        for h in range(H1):
            e.head(s, h)
        e.wo_residual(s)
    with p.nc.Block() as block:
        @block.sync
        def _(sy):
            e.s_st.wait(sy)
    return e



def declare_stageF(p, out):
    def din(name, shape, dtype=F32):
        if name not in p.t:
            p.din(name, shape, dtype)
    din("GATEW", (8, D, D)); din("UPW", (8, D, D)); din("DOWNW", (8, D, D))
    din("SGATEW", (D, D)); din("SUPW", (D, D)); din("SDOWNW", (D, D))


class StageF:
    def __init__(self, p, maskc):
        self.p = p
        self.maskc = maskc
        self.hbuf = p.sb("a_xnfm", (128, ND * 2048), BF16)
        self.xsb = p.sb("f_xs", (128, ND * 2048), BF16)
        self.down = p.sb("f_down", (128, ND * 512), BF16)
        self.wbuf = p.sb("a_wbuf", (128, 4096), BF16)
        self.sil = p.sb("f_sil", (128, 512), BF16)
        self.outst = [p.sb("f_outst0", (128, 512), F32),
                      p.sb("f_outst1", (128, 512), F32)]
        self.ps = {n: p.ps(n, (128, 512), F32) for n in ("PSA", "PSB", "PSC", "PSD")}
        self.s_ld = p.sem("f_ld")
        self.s_gld = p.sem("f_gld")
        self.s_v = p.sem("f_v")
        self.s_sc = p.sem("f_sc")
        self.s_pe = p.sem("f_pe")
        self.s_st = p.sem("f_st")

    def expert(self, e, s):
        self.load_xs(e, s)
        self.phase1(e, s)
        self.phase2(e, s)

    def load_xs(self, e, s):
        p, nc = self.p, self.p.nc
        t = p.t
        s_ld = self.s_ld
        xsrc = "XN1FM" if e == 8 else "XSFM"
        with nc.Block() as block:
            @block.sync
            def _(sy):
                for i in range(ND):
                    s_ld.inc(sy.dma_start(self.xsb[:, 2048 * i:2048 * (i + 1)],
                                          t[xsrc][s, 128 * i:128 * (i + 1), :]), 16)

    def phase1(self, e, s):
        p, nc = self.p, self.p.nc
        t = p.t
        s_ld, s_gld, s_v, s_sc, s_pe, s_st = (self.s_ld, self.s_gld, self.s_v,
                                              self.s_sc, self.s_pe, self.s_st)
        PSA, PSB, PSC, PSD = (self.ps[n] for n in ("PSA", "PSB", "PSC", "PSD"))
        shared = (e == 8)
        gsrc = ("SGATEW", None) if shared else ("GATEW", e)
        usrc = ("SUPW", None) if shared else ("UPW", e)

        for ft in range(ND):
            with nc.Block() as block:
                @block.gpsimd
                def _(g):
                    for (nm, ei), off in ((gsrc, 0), (usrc, ND * 128)):
                        w = t[nm] if ei is None else t[nm][ei]
                        sap = w[:, 128 * ft:128 * (ft + 1)].rearrange("(n p) m -> p n m", p=128)
                        dap = self.wbuf[:, off:off + ND * 128].rearrange(
                            "p (n m) -> p n m", n=ND)
                        s_gld.inc(g.dma_start(dap, sap), 16)

            for tc in range(4):
                with nc.Block() as block:
                    @block.tensor
                    def _(te):
                        s_gld.wait(te)
                        s_ld.wait(te)
                        for i in range(ND):
                            te.matmul(PSA[:], self.wbuf[:, 128 * i:128 * (i + 1)],
                                      self.xsb[:, 2048 * i + 512 * tc:2048 * i + 512 * (tc + 1)],
                                      start=(i == 0), stop=(i == ND - 1))
                        for i in range(ND):
                            ins = te.matmul(PSB[:],
                                            self.wbuf[:, ND * 128 + 128 * i:ND * 128 + 128 * (i + 1)],
                                            self.xsb[:, 2048 * i + 512 * tc:2048 * i + 512 * (tc + 1)],
                                            start=(i == 0), stop=(i == ND - 1))
                        s_pe.inc(ins)

                    @block.scalar
                    def _(sc):
                        s_pe.wait(sc)
                        s_sc.inc(sc.activation(self.sil[:], PSA[:], AF.Silu))

                    @block.vector
                    def _(v):
                        s_sc.wait(v)
                        s_v.inc(v.tensor_tensor(
                            self.hbuf[:, 2048 * ft + 512 * tc:2048 * ft + 512 * (tc + 1)],
                            self.sil[:], PSB[:], ALU.mult))

    def phase2(self, e, s):
        p, nc = self.p, self.p.nc
        t = p.t
        s_ld, s_gld, s_v, s_sc, s_pe, s_st = (self.s_ld, self.s_gld, self.s_v,
                                              self.s_sc, self.s_pe, self.s_st)
        PSA, PSB, PSC, PSD = (self.ps[n] for n in ("PSA", "PSB", "PSC", "PSD"))
        shared = (e == 8)
        dsrc = ("SDOWNW", None) if shared else ("DOWNW", e)

        nm, ei = dsrc
        wdown = t[nm] if ei is None else t[nm][ei]
        for dc in range(4):
            with nc.Block() as block:
                @block.gpsimd
                def _(g):
                    sap = wdown[:, 512 * dc:512 * (dc + 1)].rearrange(
                        "(n p) m -> p n m", p=128)
                    dap = self.down[:, 0:ND * 512].rearrange("p (n m) -> p n m", n=ND)
                    s_gld.inc(g.dma_start(dap, sap), 16)

            for tt in range(NT):
                with nc.Block() as block:
                    @block.tensor
                    def _(te):
                        s_v.wait(te)
                        s_gld.wait(te)
                        for i in range(ND):
                            ins = te.matmul(
                                PSA[:] if tt % 2 == 0 else PSB[:],
                                self.hbuf[:, 2048 * i + 128 * tt:2048 * i + 128 * (tt + 1)],
                                self.down[:, 512 * i:512 * (i + 1)],
                                start=(i == 0), stop=(i == ND - 1))
                        s_pe.inc(ins)

                    @block.vector
                    def _(v):
                        s_pe.wait(v)
                        s_st.wait(v, lag=16)
                        ps = PSA if tt % 2 == 0 else PSB
                        ob = self.outst[tt % 2]
                        if shared:
                            ins = v.tensor_copy(ob[:], ps[:])
                        else:
                            mcol = self.maskc[:, 32 * e + NT * s + tt:32 * e + NT * s + tt + 1]
                            ins = v.tensor_scalar_mul(ob[:], ps[:], mcol)
                        s_v.inc(ins)

                    @block.gpsimd
                    def _(g):
                        s_v.wait(g)
                        s_st.inc(g.dma_start(
                            t["OUT"][s, 128 * tt:128 * (tt + 1), 512 * dc:512 * (dc + 1)],
                            self.outst[tt % 2][:], accum_op=ALU.add), 16)


def emit_stageF_all(p, maskc):
    f = StageF(p, maskc)
    for e in range(9):
        for s in range(NSEQ):
            f.expert(e, s)
    with p.nc.Block() as block:
        @block.sync
        def _(sy):
            f.s_st.wait(sy)
    return f



def build_L0(stages=("A", "B", "B2", "C")):
    p = Prog()
    p.shared_pool()
    declare_stageA(p, p.dscratch)
    declare_stageB(p, p.dscratch, shared_with_A=True)
    declare_stageB2(p, p.dscratch)
    declare_stageC(p, p.dout)
    p.consts([EPS_LN, -0.5, -1.0, 1.0, EPS_L2])
    emits = {"A": emit_stageA_all, "B": emit_stageB_all,
             "B2": emit_stageB2_all, "C": emit_stageC_all}
    for sname in stages:
        p.push_scope()
        emits[sname](p)
        p.pop_scope()
    p.close()
    return p


def build_L1():
    p = Prog()
    p.shared_pool(with_scr=False)
    p.sb("d_maskc", (128, 256), F32)
    declare_stageD(p, p.dscratch)
    declare_stageE(p, p.dout)
    declare_stageF(p, p.dout)
    p.consts([EPS_LN, -0.5, -1.0, 1.0])
    p.push_scope()
    d = emit_stageD_all(p)
    p.pop_scope()
    p.push_scope()
    emit_stageE_all(p)
    p.pop_scope()
    p.push_scope()
    emit_stageF_all(p, p.sb("d_maskc", (128, 256), F32))
    p.pop_scope()
    p.close()
    return p


def build_fused():
    p = Prog()
    p.shared_pool(with_scr=False)
    p.sb("d_maskc", (128, 256), F32)
    declare_stageA(p, p.dscratch)
    declare_stageB(p, p.dscratch, shared_with_A=True)
    declare_stageB2(p, p.dscratch)
    declare_stageC(p, p.dscratch)
    declare_stageD(p, p.dscratch)
    declare_stageE(p, p.dout)
    declare_stageF(p, p.dout)
    p.consts([EPS_LN, -0.5, -1.0, 1.0, EPS_L2])
    p.push_scope()
    for i in range(6):
        p.sb("scr%d" % i, (128, 2052), F32)
    for emit in (emit_stageA_all, emit_stageB_all, emit_stageB2_all, emit_stageC_all):
        p.push_scope()
        emit(p)
        p.pop_scope()
    p.pop_scope()
    p.push_scope()
    emit_stageD_all(p)
    p.pop_scope()
    p.push_scope()
    emit_stageE_all(p)
    p.pop_scope()
    p.push_scope()
    emit_stageF_all(p, p.sb("d_maskc", (128, 256), F32))
    p.pop_scope()
    p.close()
    return p


L0_INPUTS = (STAGEA_IN + ["MASKL", "MASKUS", "MASKUI", "SEL2", "BLKID16"]
             + ["WOT", "DROW", "ONROW", "ONES1"]
             + ["LNG", "LNB", "BASES", "SLOPES", "OSCAL", "IOTAR64", "IOTAE64",
                "IOTA8E", "IOTAR8", "CLAR", "CL8", "IOTA64C"])
L1_INPUTS = ["X1C", "WQ1", "WK1", "WV1", "WR1", "WO1", "IDENT", "IDENTF",
             "MASKD0", "MASKD1", "MASKD2", "MASKD3", "ONESC", "ONESR",
             "GATEW", "UPW", "DOWNW", "SGATEW", "SUPW", "SDOWNW", "ONES1"]


def kernel(**inputs):
    f32 = np.float32
    shared = {}
    shared.update(prep_stageA(inputs))
    shared.update(prep_stageB())
    shared.update(prep_stageB2(inputs))
    shared.update(prep_stageC(inputs))
    shared["ONES1"] = np.ones((1, 128), dtype=f32)
    sharedL1 = {}
    sharedL1.update(prep_stageD(inputs))
    sharedL1.update(prep_stageE())
    sharedL1["ONESC"] = np.ones((128, 1), dtype=BF)
    sharedL1["ONESR"] = np.ones((1, 128), dtype=BF)
    sharedL1["ONES1"] = shared["ONES1"]
    sharedL1["IDENT"] = shared["IDENT"]
    sharedL1["IDENTF"] = shared["IDENTF"]
    sharedL1["GATEW"] = np.asarray(inputs["L1_ffn_gate_w"], dtype=f32)
    sharedL1["UPW"] = np.asarray(inputs["L1_ffn_up_w"], dtype=f32)
    sharedL1["DOWNW"] = np.asarray(inputs["L1_ffn_down_w"], dtype=f32)
    sharedL1["SGATEW"] = np.asarray(inputs["L1_ffn_shared_gate_w"], dtype=f32)
    sharedL1["SUPW"] = np.asarray(inputs["L1_ffn_shared_up_w"], dtype=f32)
    sharedL1["SDOWNW"] = np.asarray(inputs["L1_ffn_shared_down_w"], dtype=f32)

    x_full = np.ascontiguousarray(np.asarray(inputs["L0_attn_x"], dtype=f32))

    pf = build_fused()
    in_maps = []
    for ci in range(NCORE):
        m = {k: shared[k] for k in L0_INPUTS if k != "X"}
        for k in L1_INPUTS:
            if k != "X1C":
                m[k] = sharedL1[k]
        m["X"] = x_full[NSEQ * ci:NSEQ * (ci + 1)]
        in_maps.append(m)
    res = run_bass_kernel_spmd(pf.nc, in_maps, core_ids=list(range(NCORE)))

    out = np.empty((NSEQ * NCORE, T, D), dtype=f32)
    for ci in range(NCORE):
        out[NSEQ * ci:NSEQ * (ci + 1)] = np.asarray(res.results[ci]["OUT"])
    return out
